# version 02
# baseline (speedup 1.0000x reference)
import contextlib
import numpy as np

_PLAN = None

import concourse.bass as bass
import concourse.mybir as mybir
from concourse.bass_utils import run_bass_kernel_spmd

F32 = mybir.dt.float32
BF16 = mybir.dt.bfloat16
AF = mybir.ActivationFunctionType
ALU = mybir.AluOpType
AX = mybir.AxisListType

P = 128
D = 2048
NDCH = 16
NT = 2048
NTCH = 4
NTT = 16
INTER = 1408
NIC = 11
NEXP = 18
NPAIR = 9
NHALF = 2
HALF = 1024
TT8 = 8
NTC2 = 2
NDC4 = 4
SCALE = float(128 ** -0.5)
EPS = 1e-5
NCORES = 8

ENGINES = ("sync", "gpsimd", "vector", "scalar", "tensor")

_STAGE = "full"
_SERIAL = False
_SERIALDMA = False


class Sem:
    __slots__ = ("h", "n", "base")

    def __init__(self, h):
        self.h = h
        self.n = 0
        self.base = 0


def build_nc():
    nc = bass.Bass("TRN2", target_bir_lowering=False)

    dt = {}

    def din(name, shape, dtype=F32):
        dt[name] = nc.dram_tensor(name, list(shape), dtype, kind="ExternalInput").ap()

    din("xdev", (P, NDCH, NT))
    din("xtm", (NT, D))
    din("xpre", (P, NDCH, NT))
    din("cmask", (P, 1))
    din("wq_dev", (16, P, NDCH, P))
    din("wk_dev", (16, P, NDCH, P))
    din("wv_dev", (16, P, NDCH, P))
    din("wo_dev", (P, NDCH, D))
    din("bq_l", (P, 16))
    din("bk_l", (P, 16))
    din("bv_l", (P, 16))
    din("bo_rep", (P, D))
    din("wg_dev", (P, NDCH, 16))
    din("w1_dev", (NEXP, NIC, P, NDCH, P))
    din("w3_dev", (NEXP, NIC, P, NDCH, P))
    din("w2_dev", (NEXP, NDC4, P, NIC, 512))
    din("ident", (P, P))
    din("ones1", (1, P))
    din("epsc", (P, 1))

    out = nc.dram_tensor("out", [NT, D], F32, kind="ExternalOutput").ap()
    x1fm_scr = nc.dram_tensor("x1fm_scr", [P, NDCH, NT], BF16, kind="Internal").ap()
    y_scr = nc.dram_tensor("y_scr", [NT, D], F32, kind="Internal").ap()
    comb_hbm = nc.dram_tensor("comb_hbm", [16, NT], F32, kind="Internal").ap()

    st = contextlib.ExitStack()
    ec = st.enter_context

    ident_sb = ec(nc.sbuf_tensor("ident_sb", [P, P], F32))
    bq_sb = ec(nc.sbuf_tensor("bq_sb", [P, 16], F32))
    bk_sb = ec(nc.sbuf_tensor("bk_sb", [P, 16], F32))
    bv_sb = ec(nc.sbuf_tensor("bv_sb", [P, 16], F32))
    cmask_sb = ec(nc.sbuf_tensor("cmask_sb", [P, 1], F32))
    wg_sb = ec(nc.sbuf_tensor("wg_sb", [P, NDCH, 16], F32))
    epsc_sb = ec(nc.sbuf_tensor("epsc_sb", [P, 1], F32))
    ones_sb = ec(nc.sbuf_tensor("ones_sb", [1, P], F32))
    carry0 = ec(nc.sbuf_tensor("carry0", [P, 16], F32))
    carry_eff = ec(nc.sbuf_tensor("carry_eff", [P, 16], F32))
    zc = ec(nc.sbuf_tensor("zc", [P, 1], F32))
    combT = ec(nc.sbuf_tensor("combT", [16, NT], F32))
    scores_all = ec(nc.sbuf_tensor("scores_all", [P, NTT, 16], F32))
    comb_all = ec(nc.sbuf_tensor("comb_all", [P, NTT, 16], F32))
    r_gbuf = ec(nc.sbuf_tensor("r_gbuf", [P, NTT, 4], F32))
    r_m1 = ec(nc.sbuf_tensor("r_m1", [P, NTT, 1], F32))
    r_m2 = ec(nc.sbuf_tensor("r_m2", [P, NTT, 1], F32))
    r_t1 = ec(nc.sbuf_tensor("r_t1", [P, NTT, 16], F32))
    r_t2 = ec(nc.sbuf_tensor("r_t2", [P, NTT, 16], F32))
    r_pen = ec(nc.sbuf_tensor("r_pen", [P, NTT, 4], F32))
    r_sum = ec(nc.sbuf_tensor("r_sum", [P, NTT, 1], F32))

    ps = [ec(nc.psum_tensor("ps%d" % i, [P, 512], F32)) for i in range(8)]

    blk = ec(nc.Block())

    steps = {k: [] for k in ENGINES}
    sems = {}
    global _PLAN
    _PLAN = {k: [] for k in ENGINES}

    def sem(name):
        if name not in sems:
            s = Sem(ec(nc.semaphore(name)))
            _PLAN.setdefault("_names", {})[id(s.h)] = name
            sems[name] = s
        return sems[name]

    def alias(old_name):
        phys = sems[old_name]
        s = Sem(phys.h)
        s.base = phys.n
        s.n = phys.n
        return s

    def E(eng, fn, incs=()):
        for s, k in incs:
            s.n += k

        def run(e, fn=fn, incs=tuple(incs)):
            instr = fn(e)
            for s, k in incs:
                instr.then_inc(s.h, k)

        steps[eng].append(run)
        _PLAN[eng].append(("op", tuple((id(s.h), k) for s, k in incs)))

    def W(eng, s, val=None):
        v = s.n if val is None else s.base + val
        if v > 0:
            steps[eng].append(lambda e, h=s.h, v=v: e.wait_ge(h, v))
            _PLAN[eng].append(("wait", id(s.h), v))

    def DR(eng):
        steps[eng].append(lambda e: e.drain())

    def flush():
        for k in ENGINES:
            fns = steps[k]
            if not fns:
                continue
            deco = getattr(blk, k)

            def runner(e, fns=fns):
                for f in fns:
                    f(e)

            deco(runner)
            steps[k] = []

    sC = sem("sC")
    for name, tile in (
        ("ident", ident_sb), ("bq_l", bq_sb), ("bk_l", bk_sb), ("bv_l", bv_sb),
        ("cmask", cmask_sb), ("wg_dev", wg_sb), ("epsc", epsc_sb), ("ones1", ones_sb),
    ):
        E("sync", lambda e, t=tile, n=name: e.dma_start(out=t[:], in_=dt[n]), [(sC, 16)])
    E("vector", lambda e: e.memset(zc[:], 0.0))

    sVATT = sem("sVATT")
    sP3V = sem("sP3V")

    stCtx = contextlib.ExitStack()
    ctx_b = stCtx.enter_context(nc.sbuf_tensor("ctx_b", [P, NDCH, NT], BF16))

    with contextlib.ExitStack() as stA:
        ecA = stA.enter_context
        x_arena = ecA(nc.sbuf_tensor("x_arena", [P, NDCH, NT], BF16))
        wslot = {(pr, s): ecA(nc.sbuf_tensor("w%s%d" % (pr, s), [P, NDCH, P], BF16))
                 for pr in ("q", "k", "v") for s in (0, 1)}
        pkv_e = ecA(nc.sbuf_tensor("pkv_e", [P, NT], F32))
        phiq_e = ecA(nc.sbuf_tensor("phiq_e", [P, NT], F32))
        mt_s = {(pr, s): ecA(nc.sbuf_tensor("mt%s%d" % (pr, s), [P, 512], F32))
                for pr in ("q", "k") for s in (0, 1)}
        ex_s = {(pr, s): ecA(nc.sbuf_tensor("ex%s%d" % (pr, s), [P, 512], F32))
                for pr in ("q", "k") for s in (0, 1)}
        vp_s = ecA(nc.sbuf_tensor("vp_s", [P, 512], F32))

        sXA = sem("sXA")
        sPFX = sem("sPFX")
        sVT = sem("sVT")
        sWR = {k: sem("sWR_%s%d" % k) for k in wslot}
        wfills = {k: 0 for k in wslot}
        w_rel_at = {k: None for k in wslot}
        bank = {("q", 0): 0, ("q", 1): 3, ("k", 0): 1, ("k", 1): 4,
                ("v", 0): 2, ("v", 1): 5}
        sBR = {k: sem("sBR_%s%d" % k) for k in bank}
        sBF = {k: sem("sBF_%s%d" % k) for k in bank}
        bfills = {k: 0 for k in bank}
        sMTr = {k: sem("sMTr_%s%d" % k) for k in mt_s}
        mtfills = {k: 0 for k in mt_s}
        sEXr = {k: sem("sEXr_%s%d" % k) for k in ex_s}
        sEXf = {k: sem("sEXf_%s%d" % k) for k in ex_s}
        exfills = {k: 0 for k in ex_s}

        def load_w(pr, ech, src):
            s = ech % 2
            k = (pr, s)
            if _SERIALDMA:
                W("gpsimd", sVT)
            if w_rel_at[k] is not None:
                W("gpsimd", w_rel_at[k][0], w_rel_at[k][1])
            E("gpsimd",
              lambda e, t=wslot[k], n=src, ech=ech: e.dma_start(out=t[:], in_=dt[n][ech]),
              [(sWR[k], 16)])
            wfills[k] += 1

        def proj_mms(pr, ech, g, tch, release_w):
            s = ech % 2
            par = g % 2
            bnk = bank[(pr, par)]
            W("tensor", sWR[(pr, s)], 16 * wfills[(pr, s)])
            if bfills[(pr, par)] > 0:
                W("tensor", sBF[(pr, par)], bfills[(pr, par)])
            for dch in range(NDCH):
                E("tensor",
                  lambda e, bnk=bnk, w=wslot[(pr, s)], dch=dch, tch=tch:
                  e.matmul(ps[bnk][:, :], w[:, dch, :],
                           x_arena[:, dch, tch * 512:(tch + 1) * 512],
                           start=(dch == 0), stop=(dch == NDCH - 1)),
                  [(sBR[(pr, par)], 1)] if dch == NDCH - 1 else ())
            bfills[(pr, par)] += 1
            if release_w:
                w_rel_at[(pr, s)] = (sBF[(pr, par)], bfills[(pr, par)])

        def phi_chain(pr, ech, g, bias_sb, dest):
            par = g % 2
            bnk = bank[(pr, par)]
            mk = (pr, par)
            W("vector", sBR[(pr, par)], bfills[(pr, par)])
            if mtfills[mk] > 0:
                W("vector", sEXr[mk], mtfills[mk])
            E("vector",
              lambda e, bnk=bnk, mk=mk, b=bias_sb, ech=ech:
              e.tensor_scalar(mt_s[mk][:, :], ps[bnk][:, :], b[:, ech:ech + 1], 0.0,
                              ALU.add, ALU.min),
              [(sMTr[mk], 1)])
            mtfills[mk] += 1
            W("scalar", sMTr[mk], mtfills[mk])
            if exfills[mk] > 0:
                W("scalar", sEXf[mk], exfills[mk])
            E("scalar",
              lambda e, mk=mk: e.activation(ex_s[mk][:, :], mt_s[mk][:, :], AF.Exp),
              [(sEXr[mk], 1)])
            exfills[mk] += 1
            E("vector",
              lambda e, bnk=bnk, b=bias_sb, ech=ech, d=dest:
              e.tensor_scalar(d, ps[bnk][:, :], b[:, ech:ech + 1], 0.0, ALU.add, ALU.max),
              [(sBF[(pr, par)], 1)])
            W("vector", sEXr[mk], exfills[mk])
            E("vector",
              lambda e, mk=mk, d=dest: e.tensor_tensor(d, d, ex_s[mk][:, :], ALU.add),
              [(sEXf[mk], 1)])

        def v_evac(ech, g):
            par = g % 2
            bnk = bank[("v", par)]
            W("vector", sBR[("v", par)], bfills[("v", par)])
            E("vector",
              lambda e, bnk=bnk, ech=ech:
              e.tensor_scalar(vp_s[:, :], ps[bnk][:, :], bv_sb[:, ech:ech + 1],
                              None, ALU.add),
              [(sBF[("v", par)], 1)])

        E("gpsimd", lambda e: e.dma_start(out=x_arena[:], in_=dt["xpre"]), [(sXA, 16)])
        load_w("k", 0, "wk_dev")
        load_w("v", 0, "wv_dev")
        g = 0
        for ech in range(16):
            if ech + 1 < 16:
                load_w("k", ech + 1, "wk_dev")
                load_w("v", ech + 1, "wv_dev")
            if ech == 0:
                W("tensor", sXA, 16)
                W("vector", sC, 16 * 8)
            for tch in range(NTCH):
                rel = (tch == NTCH - 1)
                if _SERIAL:
                    W("tensor", sVT)
                proj_mms("k", ech, g, tch, rel)
                proj_mms("v", ech, g, tch, rel)
                dest = pkv_e[:, tch * 512:(tch + 1) * 512]
                phi_chain("k", ech, g, bk_sb, dest)
                v_evac(ech, g)
                E("vector", lambda e, d=dest: e.tensor_tensor(d, d, vp_s[:, :], ALU.mult),
                  [(sVT, 1)])
                g += 1
            E("vector",
              lambda e, ech=ech:
              e.reduce_sum(carry0[:, ech:ech + 1], pkv_e[:, :], axis=AX.X))
        DR("vector")
        E("vector",
          lambda e: e.tensor_scalar(carry_eff[:, :], carry0[:, :], cmask_sb[:, 0:1],
                                    None, ALU.mult),
          [(sPFX, 1)])
        DR("vector")

        W("gpsimd", sPFX, 1)
        E("gpsimd", lambda e: e.dma_start(out=x_arena[:], in_=dt["xdev"]), [(sXA, 16)])
        load_w("q", 0, "wq_dev")
        load_w("k", 0, "wk_dev")
        load_w("v", 0, "wv_dev")
        for ech in range(16):
            if ech + 1 < 16:
                load_w("q", ech + 1, "wq_dev")
                load_w("k", ech + 1, "wk_dev")
                load_w("v", ech + 1, "wv_dev")
            if ech == 0:
                W("tensor", sXA, 32)
            for tch in range(NTCH):
                rel = (tch == NTCH - 1)
                if _SERIAL:
                    W("tensor", sVT)
                proj_mms("q", ech, g, tch, rel)
                proj_mms("k", ech, g, tch, rel)
                proj_mms("v", ech, g, tch, rel)
                phi_chain("q", ech, g, bq_sb, phiq_e[:, tch * 512:(tch + 1) * 512])
                dest = pkv_e[:, tch * 512:(tch + 1) * 512]
                phi_chain("k", ech, g, bk_sb, dest)
                v_evac(ech, g)
                E("vector", lambda e, d=dest: e.tensor_tensor(d, d, vp_s[:, :], ALU.mult),
                  [(sVT, 1)])
                g += 1
            E("vector",
              lambda e, ech=ech:
              e.tensor_tensor_scan(pkv_e[:, :], pkv_e[:, :],
                                   zc[:, 0:1].broadcast_to([P, NT]),
                                   carry_eff[:, ech:ech + 1], ALU.add, ALU.add))
            DR("vector")
            E("vector",
              lambda e, ech=ech:
              e.scalar_tensor_tensor(ctx_b[:, ech, :], phiq_e[:, :], SCALE, pkv_e[:, :],
                                     ALU.mult, ALU.mult),
              [(sVATT, 1)])

        if _STAGE == "ctx":
            sF = sem("sF")
            W("gpsimd", sVATT, 16)
            E("gpsimd",
              lambda e: e.dma_start(out=out.rearrange("(c p) t -> p c t", p=P),
                                    in_=ctx_b[:]), [(sF, 16)])
            W("sync", sF, 16)
            steps["sync"].append(lambda e: e.nop())
        flush()
    if _STAGE == "ctx":
        stCtx.close()
        st.close()
        return nc

    with contextlib.ExitStack() as stB:
        ecB = stB.enter_context
        woT_b = ecB(nc.sbuf_tensor("woT_b", [P, NDCH, D], BF16))
        borep_sb = ecB(nc.sbuf_tensor("borep_sb", [P, D], F32))
        u_t = ecB(nc.sbuf_tensor("u_t", [P, D], F32))
        xbo = ecB(nc.sbuf_tensor("xbo", [P, D], F32))
        xtm_t = ecB(nc.sbuf_tensor("xtm_t", [P, D], F32))
        x1tm_t = ecB(nc.sbuf_tensor("x1tm_t", [P, D], F32))
        x1f32 = ecB(nc.sbuf_tensor("x1f32", [P, NDCH, P], F32))
        x1b16 = ecB(nc.sbuf_tensor("x1b16", [P, NDCH, P], BF16))
        msum = ecB(nc.sbuf_tensor("msum", [P, 4], F32))
        mred = ecB(nc.sbuf_tensor("mred", [P, 1], F32))
        meanb = ecB(nc.sbuf_tensor("meanb", [P, 1], F32))
        m2b = ecB(nc.sbuf_tensor("m2b", [P, 1], F32))
        ssqb = ecB(nc.sbuf_tensor("ssqb", [P, 1], F32))
        varb = ecB(nc.sbuf_tensor("varb", [P, 1], F32))
        lnvb = ecB(nc.sbuf_tensor("lnvb", [P, 1], F32))
        rstdb = ecB(nc.sbuf_tensor("rstdb", [P, 1], F32))

        sWO = sem("sWO")
        sBOR = sem("sBOR")
        sXT = sem("sXT")
        sXTf = sem("sXTf")
        xt_fills = 0
        sX1r = sem("sX1r")
        sX1f = sem("sX1f")
        x1_fills = 0
        sPSA = [sem("sPSA%d" % i) for i in range(4)]
        sPSAf = [sem("sPSAf%d" % i) for i in range(4)]
        psa_fills = [0, 0, 0, 0]
        psa_fill_at = {}
        sTRr = [sem("sTRr%d" % i) for i in (0, 1)]
        sTRE = sem("sTRE")
        tr_fills = [0, 0]
        tr_rel_at = {0: 0, 1: 0}
        sPEG = sem("sPEG")
        sPSG = sem("sPSG")
        sPSGf = sem("sPSGf")
        sVLN = sem("sVLN")
        sRSTD = sem("sRSTD")
        sYW = sem("sYW")
        sXW = sem("sXW")
        sX1B = sem("sX1B")

        W("gpsimd", sVATT, 16)
        for cpart in range(4):
            E("gpsimd",
              lambda e, c=cpart:
              e.dma_start(out=woT_b[:, c * 4:(c + 1) * 4, :],
                          in_=dt["wo_dev"][:, c * 4:(c + 1) * 4, :]),
              [(sWO, 16)])
        W("sync", sVATT, 16)
        E("sync", lambda e: e.dma_start(out=borep_sb[:], in_=dt["bo_rep"]), [(sBOR, 16)])
        E("sync", lambda e: e.dma_start(out=xtm_t[:], in_=dt["xtm"][0:P, :]), [(sXT, 16)])
        xt_fills += 1

        def emit_wo(tt):
            if tt == 0:
                W("tensor", sWO, 64)
                W("tensor", sVATT, 16)
            for dc in range(NDC4):
                par = (tt * NDC4 + dc) % 4
                if psa_fills[par] > 0:
                    W("tensor", sPSAf[par], psa_fills[par])
                for dch in range(NDCH):
                    E("tensor",
                      lambda e, par=par, dch=dch, tt=tt, dc=dc:
                      e.matmul(ps[par][:, :], ctx_b[:, dch, tt * P:(tt + 1) * P],
                               woT_b[:, dch, dc * 512:(dc + 1) * 512],
                               start=(dch == 0), stop=(dch == NDCH - 1)),
                      [(sPSA[par], 1)] if dch == NDCH - 1 else ())
                psa_fills[par] += 1
                psa_fill_at[(tt, dc)] = psa_fills[par]

        def emit_ln1(tt):
            nonlocal xt_fills, x1_fills
            W("vector", sXT, 16 * xt_fills)
            if tt == 0:
                W("vector", sBOR, 16)
            E("vector",
              lambda e: e.tensor_tensor(xbo[:, :], xtm_t[:, :], borep_sb[:, :], ALU.add),
              [(sXTf, 1)])
            if tt + 1 < NTT:
                W("sync", sXTf, xt_fills)
                E("sync",
                  lambda e, tt=tt:
                  e.dma_start(out=xtm_t[:], in_=dt["xtm"][(tt + 1) * P:(tt + 2) * P, :]),
                  [(sXT, 16)])
                xt_fills += 1
            for dc in range(NDC4):
                par = (tt * NDC4 + dc) % 4
                W("vector", sPSA[par], psa_fill_at[(tt, dc)])
                E("vector",
                  lambda e, par=par, dc=dc:
                  e.scalar_tensor_tensor(u_t[:, dc * 512:(dc + 1) * 512], ps[par][:, :],
                                         0.0, xbo[:, dc * 512:(dc + 1) * 512],
                                         ALU.add, ALU.add, accum_out=msum[:, dc:dc + 1]),
                  [(sPSAf[par], 1)])
            DR("vector")
            E("vector", lambda e: e.reduce_sum(mred[:, 0:1], msum[:, :], axis=AX.X))
            DR("vector")
            E("vector", lambda e: e.tensor_scalar(meanb[:, 0:1], mred[:, 0:1],
                                                  1.0 / D, None, ALU.mult))
            E("vector",
              lambda e: e.scalar_tensor_tensor(xbo[:, :], u_t[:, :], 1.0, u_t[:, :],
                                               ALU.mult, ALU.mult,
                                               accum_out=ssqb[:, 0:1]))
            DR("vector")
            E("vector", lambda e: e.tensor_tensor(m2b[:, 0:1], meanb[:, 0:1],
                                                  meanb[:, 0:1], ALU.mult))
            E("vector",
              lambda e: e.scalar_tensor_tensor(varb[:, 0:1], ssqb[:, 0:1], 1.0 / D,
                                               m2b[:, 0:1], ALU.mult, ALU.subtract))
            E("vector",
              lambda e: e.scalar_tensor_tensor(varb[:, 0:1], ssqb[:, 0:1], 1.0 / D,
                                               m2b[:, 0:1], ALU.mult, ALU.subtract),
              [(sVLN, 1)])
            W("scalar", sVLN)
            E("scalar", lambda e: e.activation(lnvb[:, 0:1], varb[:, 0:1], AF.Ln,
                                               bias=epsc_sb[:, 0:1]))
            DR("scalar")
            E("scalar", lambda e: e.activation(rstdb[:, 0:1], lnvb[:, 0:1], AF.Exp,
                                               scale=-0.5))
            E("scalar", lambda e: e.activation(rstdb[:, 0:1], lnvb[:, 0:1], AF.Exp,
                                               scale=-0.5), [(sRSTD, 1)])
            if x1_fills > 0:
                W("vector", sYW, 16 * x1_fills)
            W("vector", sRSTD)
            E("vector",
              lambda e:
              e.tensor_scalar(x1tm_t[:, :], u_t[:, :], meanb[:, 0:1], rstdb[:, 0:1],
                              ALU.subtract, ALU.mult),
              [(sX1r, 1)])
            x1_fills += 1
            W("sync", sX1r, x1_fills)
            E("sync",
              lambda e, tt=tt:
              e.dma_start(out=y_scr[tt * P:(tt + 1) * P, :], in_=x1tm_t[:, :]),
              [(sYW, 16)])

        def emit_tr_gate(tt):
            W("tensor", sX1r, x1_fills)
            if tt > 0:
                W("tensor", sPSGf, tt)
            for grp in range(4):
                tp = grp % 2
                if tr_fills[tp] > 0:
                    W("tensor", sTRE, tr_rel_at[tp])
                for j in range(4):
                    dch = grp * 4 + j
                    E("tensor",
                      lambda e, tp=tp, j=j, dch=dch:
                      e.transpose(ps[4 + tp][:, j * P:(j + 1) * P],
                                  x1tm_t[:, dch * P:(dch + 1) * P], ident_sb[:]),
                      [(sTRr[tp], 1)] if j == 3 else ())
                tr_fills[tp] += 1
                W("vector", sTRr[tp], tr_fills[tp])
                E("vector",
                  lambda e, tp=tp, grp=grp:
                  e.tensor_copy(x1f32[:, grp * 4:(grp + 1) * 4, :], ps[4 + tp][:, :]),
                  [(sTRE, 1)])
                tr_rel_at[tp] = sTRE.n
            if tt > 0:
                W("vector", sXW, 16 * tt)
            E("vector", lambda e: e.tensor_copy(x1b16[:, :, :], x1f32[:, :, :]),
              [(sX1B, 1)])
            W("sync", sX1B)
            E("sync",
              lambda e, tt=tt:
              e.dma_start(out=x1fm_scr[:, :, tt * P:(tt + 1) * P], in_=x1b16[:, :, :]),
              [(sXW, 16)])
            W("tensor", sTRE)
            for dch in range(NDCH):
                E("tensor",
                  lambda e, dch=dch:
                  e.matmul(ps[7][:, 0:16], x1f32[:, dch, :], wg_sb[:, dch, :],
                           start=(dch == 0), stop=(dch == NDCH - 1)),
                  [(sPSG, 1)] if dch == NDCH - 1 else ())
            W("vector", sPSG)
            E("vector",
              lambda e, tt=tt: e.tensor_copy(scores_all[:, tt, :], ps[7][:, 0:16]),
              [(sPSGf, 1)])

        emit_wo(0)
        for tt in range(NTT):
            if tt + 1 < NTT:
                emit_wo(tt + 1)
            emit_ln1(tt)
            emit_tr_gate(tt)

        sEXPR = sem("sEXPR")
        sEXPRf = sem("sEXPRf")
        sROUT = sem("sROUT")
        s4 = scores_all[:, :, :].rearrange("p t (g f) -> p t g f", f=4)
        t14 = r_t1[:, :, :].rearrange("p t (g f) -> p t g f", f=4)
        t24 = r_t2[:, :, :].rearrange("p t (g f) -> p t g f", f=4)

        def RV(fn, incs=()):
            E("vector", fn, incs)

        RV(lambda e: e.reduce_max(r_m1[:, :, 0:1], scores_all[:, :, :], axis=AX.X))
        DR("vector")
        RV(lambda e: e.tensor_scalar(r_m1[:, :, 0:1], r_m1[:, :, 0:1], -1.0, None, ALU.mult))
        RV(lambda e: e.tensor_tensor(r_t1[:, :, :], scores_all[:, :, :],
                                     r_m1[:, :, 0:1].broadcast_to([P, NTT, 16]), ALU.add),
           [(sEXPR, 1)])
        W("scalar", sEXPR)
        E("scalar",
          lambda e: e.activation(r_t2[:, :, :].rearrange("p t e -> p (t e)"),
                                 r_t1[:, :, :].rearrange("p t e -> p (t e)"), AF.Exp),
          [(sEXPRf, 1)])
        W("vector", sEXPRf)
        RV(lambda e: e.reduce_sum(r_sum[:, :, 0:1], r_t2[:, :, :], axis=AX.X))
        DR("vector")
        RV(lambda e: e.reciprocal(r_sum[:, :, 0:1].rearrange("p t o -> p (t o)"),
                                  r_sum[:, :, 0:1].rearrange("p t o -> p (t o)")))
        DR("vector")
        RV(lambda e: e.tensor_tensor(scores_all[:, :, :], r_t2[:, :, :],
                                     r_sum[:, :, 0:1].broadcast_to([P, NTT, 16]), ALU.mult))
        RV(lambda e: e.tensor_tensor(r_gbuf[:, :, :], s4[:, :, :, 0], s4[:, :, :, 1], ALU.max))
        RV(lambda e: e.tensor_tensor(r_pen[:, :, :], s4[:, :, :, 2], s4[:, :, :, 3], ALU.max))
        RV(lambda e: e.tensor_tensor(r_gbuf[:, :, :], r_gbuf[:, :, :], r_pen[:, :, :], ALU.max))
        RV(lambda e: e.reduce_max(r_m1[:, :, 0:1], r_gbuf[:, :, :], axis=AX.X))
        DR("vector")
        RV(lambda e: e.tensor_tensor(r_pen[:, :, :], r_gbuf[:, :, :],
                                     r_m1[:, :, 0:1].broadcast_to([P, NTT, 4]), ALU.is_ge))
        RV(lambda e: e.scalar_tensor_tensor(r_pen[:, :, :], r_pen[:, :, :], -1e30,
                                            r_gbuf[:, :, :], ALU.mult, ALU.add))
        RV(lambda e: e.reduce_max(r_m2[:, :, 0:1], r_pen[:, :, :], axis=AX.X))
        DR("vector")
        RV(lambda e: e.tensor_tensor(r_gbuf[:, :, :], r_gbuf[:, :, :],
                                     r_m2[:, :, 0:1].broadcast_to([P, NTT, 4]), ALU.is_ge))
        RV(lambda e: e.tensor_scalar(r_pen[:, :, :], r_gbuf[:, :, :], 1.0, 1e30,
                                     ALU.subtract, ALU.mult))
        for f in range(4):
            RV(lambda e, f=f: e.tensor_tensor(t14[:, :, :, f], s4[:, :, :, f],
                                              r_pen[:, :, :], ALU.add))
        for _ in range(3):
            RV(lambda e: e.reduce_max(r_m1[:, :, 0:1], r_t1[:, :, :], axis=AX.X))
            DR("vector")
            RV(lambda e: e.tensor_tensor(r_t2[:, :, :], r_t1[:, :, :],
                                         r_m1[:, :, 0:1].broadcast_to([P, NTT, 16]),
                                         ALU.is_ge))
            RV(lambda e: e.scalar_tensor_tensor(r_t1[:, :, :], r_t2[:, :, :], -1e30,
                                                r_t1[:, :, :], ALU.mult, ALU.add))
        RV(lambda e: e.reduce_max(r_m2[:, :, 0:1], r_t1[:, :, :], axis=AX.X))
        DR("vector")
        for f in range(4):
            RV(lambda e, f=f: e.tensor_tensor(t24[:, :, :, f], s4[:, :, :, f],
                                              r_pen[:, :, :], ALU.add))
        RV(lambda e: e.tensor_tensor(r_t2[:, :, :], r_t2[:, :, :],
                                     r_m2[:, :, 0:1].broadcast_to([P, NTT, 16]), ALU.is_ge))
        RV(lambda e: e.tensor_tensor(comb_all[:, :, :], scores_all[:, :, :],
                                     r_t2[:, :, :], ALU.mult),
           [(sROUT, 1)])

        sCTr = [sem("sCTr%d" % i) for i in (0, 1)]
        ct_fills = [0, 0]
        ct_rel_at = {0: 0, 1: 0}
        sCBW = sem("sCBW")
        sCB = sem("sCB")
        W("tensor", sROUT)
        for tt in range(NTT):
            par = tt % 2
            if ct_fills[par] > 0:
                W("tensor", sCBW, ct_rel_at[par])
            E("tensor",
              lambda e, par=par, tt=tt:
              e.transpose(ps[4 + par][0:16, 0:P], comb_all[:, tt, :], ident_sb[:]),
              [(sCTr[par], 1)])
            ct_fills[par] += 1
            W("vector", sCTr[par], ct_fills[par])
            E("vector",
              lambda e, par=par, tt=tt:
              e.tensor_copy(combT[0:16, tt * P:(tt + 1) * P], ps[4 + par][0:16, 0:P]),
              [(sCBW, 1)])
            ct_rel_at[par] = sCBW.n
        W("sync", sCBW, NTT)
        E("sync", lambda e: e.dma_start(out=comb_hbm, in_=combT[:, :]), [(sCB, 16)])
        E("vector", lambda e: e.engine_nop(), [(sP3V, 1)])

        if _STAGE in ("x1", "comb"):
            sF = sem("sF")
            if _STAGE == "x1":
                W("sync", sYW, 16 * NTT)
                E("sync", lambda e: e.dma_start(out=out, in_=y_scr), [(sF, 16)])
            else:
                W("sync", sem("sCB"), 16)
                E("sync",
                  lambda e: e.dma_start(
                      out=out[0:P, 0:NTT * 16].rearrange("p (t e) -> p t e", e=16),
                      in_=comb_all[:, :, :]), [(sF, 16)])
            W("sync", sF, 16)
            steps["sync"].append(lambda e: e.nop())
        flush()
    stCtx.close()
    if _STAGE in ("x1", "comb"):
        st.close()
        return nc

    sACCY = None
    with contextlib.ExitStack() as stM:
        ecM = stM.enter_context
        x1h_b = ecM(nc.sbuf_tensor("x1h_b", [P, NDCH, HALF], BF16))
        h_pair = ecM(nc.sbuf_tensor("h_pair", [P, 2, NIC, HALF], BF16))
        w13 = {(w, s): ecM(nc.sbuf_tensor("w%d_%d" % (w, s), [P, NDCH, P], BF16))
               for w in (1, 3) for s in (0, 1)}
        w2c = [ecM(nc.sbuf_tensor("w2c%d" % s, [P, NIC, 512], BF16)) for s in range(3)]
        c_rep = ecM(nc.sbuf_tensor("c_rep", [P, 2, HALF], F32))
        sa_s = [ecM(nc.sbuf_tensor("sa_s%d" % i, [P, 512], F32)) for i in (0, 1)]
        cr_s = [ecM(nc.sbuf_tensor("cr_s%d" % i, [1, 2 * NT], F32)) for i in (0, 1)]
        bp_s = ecM(nc.sbuf_tensor("bp_s", [P, 512], F32))
        yev = [ecM(nc.sbuf_tensor("yev%d" % i, [P, 512], F32)) for i in (0, 1)]

        sX1H = alias("sXA")
        sW13r = {(1, 0): alias("sWR_q0"), (1, 1): alias("sWR_q1"),
                 (3, 0): alias("sWR_k0"), (3, 1): alias("sWR_k1")}
        w13_fills = {k: 0 for k in w13}
        w13_rel_at = [None, None]
        sW2r = [alias("sWR_v0"), alias("sWR_v1"), alias("sBR_q0")]
        w2_fills = [0, 0, 0]
        w2_rel_at = [None, None, None]
        sPAr = [alias("sBR_k0"), alias("sBR_k1")]
        pa_fills = [0, 0]
        sPBr = [alias("sBR_v0"), alias("sBR_v1")]
        sPBf = [alias("sBF_k0"), alias("sBF_k1")]
        pb_fills = [0, 0]
        pb_rel_at = [None, None]
        sPYr = [alias("sBF_v0"), alias("sBF_v1")]
        py_fills = [0, 0]
        sPCr = alias("sEXr_q0")
        sPCf = alias("sEXr_q1")
        pc_fills = 0
        sCRr = [alias("sEXr_k0"), alias("sEXr_k1")]
        cr_fills = [0, 0]
        cr_rel_at = [None, None]
        sSAr = [alias("sMTr_q0"), alias("sMTr_q1")]
        sa_fills = [0, 0]
        sa_rel_at = [0, 0]
        sYEr = [alias("sMTr_k0"), alias("sMTr_k1")]
        ye_fills = [0, 0]
        ye_rel_at = [0, 0]
        sHV = alias("sEXf_k0")
        sACCY = alias("sEXf_k1")

        chunk_seq = []
        for half in range(NHALF):
            for pair in range(NPAIR):
                for e01 in (0, 1):
                    eidx = 2 * pair + e01
                    for ic in range(NIC):
                        chunk_seq.append((eidx, ic))
        n_chunks = len(chunk_seq)

        def load_w13(j, first=False):
            eidx, ic = chunk_seq[j]
            slot = j % 2
            if first:
                W("gpsimd", sP3V, 1)
                W("gpsimd", sXW, 16 * NTT)
                W("gpsimd", sYW, 16 * NTT)
            elif w13_rel_at[slot] is not None:
                W("gpsimd", w13_rel_at[slot][0], w13_rel_at[slot][1])
            for w in (1, 3):
                k = (w, slot)
                E("gpsimd",
                  lambda e, k=k, w=w, eidx=eidx, ic=ic:
                  e.dma_start(out=w13[k][:], in_=dt["w%d_dev" % w][eidx, ic]),
                  [(sW13r[k], 16)])
                w13_fills[k] += 1

        w2_next = 0

        def load_w2(eidx, dch):
            nonlocal w2_next
            s = w2_next % 3
            w2_next += 1
            if w2_rel_at[s] is not None:
                W("gpsimd", w2_rel_at[s][0], w2_rel_at[s][1])
            E("gpsimd",
              lambda e, s=s, eidx=eidx, dch=dch:
              e.dma_start(out=w2c[s][:], in_=dt["w2_dev"][eidx, dch]),
              [(sW2r[s], 16)])
            w2_fills[s] += 1
            return s

        load_w13(0, first=True)
        ab_group = 0
        y_group = 0
        jchunk = 0

        for half in range(NHALF):
            if half == 0:
                W("sync", sXW, 16 * NTT)
                W("sync", sYW, 16 * NTT)
                W("sync", sP3V, 1)
            else:
                W("sync", sHV, sHV.n - sHV.base)
            E("sync",
              lambda e, half=half:
              e.dma_start(out=x1h_b[:],
                          in_=x1fm_scr[:, :, half * HALF:(half + 1) * HALF]),
              [(sX1H, 16)])
            if half == 0:
                W("gpsimd", sYW, 16 * NTT)

            for pair in range(NPAIR):
                shared = (pair == NPAIR - 1)
                e0, e1 = 2 * pair, 2 * pair + 1
                if not shared:
                    crslot = (half * NPAIR + pair) % 2
                    if cr_rel_at[crslot] is not None:
                        W("sync", cr_rel_at[crslot][0], cr_rel_at[crslot][1])
                    if half == 0 and pair == 0:
                        W("sync", sCB, 16)
                    for e01, eidx in ((0, e0), (1, e1)):
                        E("sync",
                          lambda e, s=crslot, e01=e01, eidx=eidx:
                          e.dma_start(out=cr_s[s][0:1, e01 * NT:(e01 + 1) * NT],
                                      in_=comb_hbm[eidx:eidx + 1, :]),
                          [(sCRr[crslot], 16)])
                    cr_fills[crslot] += 1
                    W("tensor", sCRr[crslot], 32 * cr_fills[crslot])
                    nmm = 0
                    for e01 in (0, 1):
                        for tc in range(NTC2):
                            if pc_fills > 0:
                                W("tensor", sPCf, pc_fills)
                            off = e01 * NT + half * HALF + tc * 512
                            nmm += 1
                            E("tensor",
                              lambda e, s=crslot, off=off:
                              e.matmul(ps[6][:, :], ones_sb[0:1, :],
                                       cr_s[s][0:1, off:off + 512],
                                       start=True, stop=True),
                              [(sPCr, 1)])
                            pc_fills += 1
                            W("vector", sPCr, pc_fills)
                            E("vector",
                              lambda e, e01=e01, tc=tc:
                              e.tensor_copy(c_rep[:, e01, tc * 512:(tc + 1) * 512],
                                            ps[6][:, :]),
                              [(sPCf, 1)])
                            if nmm == 4:
                                cr_rel_at[crslot] = (sPCf, sPCf.n - sPCf.base)
                for e01 in (0, 1):
                    for ic in range(NIC):
                        slot = jchunk % 2
                        if jchunk + 1 < n_chunks:
                            load_w13(jchunk + 1)
                        k1 = (1, slot)
                        k3 = (3, slot)
                        for tc in range(NTC2):
                            par = ab_group % 2
                            W("tensor", sW13r[k1], 16 * w13_fills[k1])
                            if ab_group == 0:
                                W("tensor", sX1H, 16)
                            if half == 1 and pair == 0 and e01 == 0 and ic == 0 and tc == 0:
                                W("tensor", sX1H, 32)
                            if pa_fills[par] > 0:
                                W("tensor", sSAr[par], pa_fills[par])
                            for dch in range(NDCH):
                                E("tensor",
                                  lambda e, par=par, k1=k1, dch=dch, tc=tc:
                                  e.matmul(ps[0 + 2 * par][:, :], w13[k1][:, dch, :],
                                           x1h_b[:, dch, tc * 512:(tc + 1) * 512],
                                           start=(dch == 0), stop=(dch == NDCH - 1)),
                                  [(sPAr[par], 1)] if dch == NDCH - 1 else ())
                            pa_fills[par] += 1
                            W("tensor", sW13r[k3], 16 * w13_fills[k3])
                            if pb_fills[par] > 0 and pb_rel_at[par] is not None:
                                W("tensor", pb_rel_at[par][0], pb_rel_at[par][1])
                            for dch in range(NDCH):
                                E("tensor",
                                  lambda e, par=par, k3=k3, dch=dch, tc=tc:
                                  e.matmul(ps[1 + 2 * par][:, :], w13[k3][:, dch, :],
                                           x1h_b[:, dch, tc * 512:(tc + 1) * 512],
                                           start=(dch == 0), stop=(dch == NDCH - 1)),
                                  [(sPBr[par], 1)] if dch == NDCH - 1 else ())
                            pb_fills[par] += 1
                            sslot = ab_group % 2
                            W("scalar", sPAr[par], pa_fills[par])
                            if sa_fills[sslot] > 0:
                                W("scalar", sHV, sa_rel_at[sslot])
                            E("scalar",
                              lambda e, par=par, ss=sslot:
                              e.activation(sa_s[ss][:, :], ps[0 + 2 * par][:, :], AF.Silu),
                              [(sSAr[sslot], 1)])
                            sa_fills[sslot] += 1
                            hdst = h_pair[:, e01, ic, tc * 512:(tc + 1) * 512]
                            W("vector", sPBr[par], pb_fills[par])
                            if shared:
                                W("vector", sSAr[sslot], sa_fills[sslot])
                                E("vector",
                                  lambda e, par=par, ss=sslot, hd=hdst:
                                  e.tensor_tensor(hd, sa_s[ss][:, :],
                                                  ps[1 + 2 * par][:, :], ALU.mult),
                                  [(sHV, 1)])
                                pb_rel_at[par] = (sHV, sHV.n - sHV.base)
                            else:
                                E("vector",
                                  lambda e, par=par, e01=e01, tc=tc:
                                  e.tensor_tensor(bp_s[:, :], ps[1 + 2 * par][:, :],
                                                  c_rep[:, e01, tc * 512:(tc + 1) * 512],
                                                  ALU.mult),
                                  [(sPBf[par], 1)])
                                pb_rel_at[par] = (sPBf[par], sPBf[par].n - sPBf[par].base)
                                W("vector", sSAr[sslot], sa_fills[sslot])
                                E("vector",
                                  lambda e, ss=sslot, hd=hdst:
                                  e.tensor_tensor(hd, sa_s[ss][:, :], bp_s[:, :],
                                                  ALU.mult),
                                  [(sHV, 1)])
                            sa_rel_at[sslot] = sHV.n - sHV.base
                            if tc == NTC2 - 1:
                                w13_rel_at[slot] = (sHV, sHV.n - sHV.base)
                            ab_group += 1
                        jchunk += 1
                hv_target = sHV.n - sHV.base
                for dch in range(NDC4):
                    s0 = load_w2(e0, dch)
                    s1 = load_w2(e1, dch)
                    for tt in range(TT8):
                        ypar = y_group % 2
                        if py_fills[ypar] > 0:
                            W("tensor", sYEr[ypar], py_fills[ypar])
                        if tt == 0:
                            W("tensor", sW2r[s0], 16 * w2_fills[s0])
                            W("tensor", sW2r[s1], 16 * w2_fills[s1])
                            W("tensor", sHV, hv_target)
                        n_mm = 2 * NIC
                        mm_i = 0
                        for e01, ws in ((0, s0), (1, s1)):
                            for ic in range(NIC):
                                first = (mm_i == 0)
                                last = (mm_i == n_mm - 1)
                                E("tensor",
                                  lambda e, yp=ypar, e01=e01, ic=ic, tt=tt, ws=ws,
                                  f=first, l=last:
                                  e.matmul(ps[4 + yp][:, :],
                                           h_pair[:, e01, ic, tt * P:(tt + 1) * P],
                                           w2c[ws][:, ic, :], start=f, stop=l),
                                  [(sPYr[ypar], 1)] if last else ())
                                mm_i += 1
                        py_fills[ypar] += 1
                        yslot = y_group % 2
                        W("vector", sPYr[ypar], py_fills[ypar])
                        if ye_fills[yslot] > 0:
                            W("vector", sACCY, 16 * ye_rel_at[yslot])
                        E("vector",
                          lambda e, yp=ypar, ys=yslot:
                          e.tensor_copy(yev[ys][:, :], ps[4 + yp][:, :]),
                          [(sYEr[yslot], 1)])
                        ye_fills[yslot] += 1
                        W("gpsimd", sYEr[yslot], ye_fills[yslot])
                        row0 = half * HALF + tt * P
                        E("gpsimd",
                          lambda e, ys=yslot, r=row0, dch=dch:
                          e.dma_start(out=y_scr[r:r + P, dch * 512:(dch + 1) * 512],
                                      in_=yev[ys][:, :], accum_op=ALU.add),
                          [(sACCY, 16)])
                        ye_rel_at[yslot] = (sACCY.n - sACCY.base) // 16
                        if tt == TT8 - 1:
                            rel = (sYEr[ypar], sYEr[ypar].n - sYEr[ypar].base)
                            w2_rel_at[s0] = rel
                            w2_rel_at[s1] = rel
                        y_group += 1
                W("gpsimd", sACCY)
        flush()
        sACCY_out = sACCY
    if _STAGE == "y":
        sFy = sem("sFy")
        W("sync", sACCY_out)
        E("sync", lambda e: e.dma_start(out=out, in_=y_scr), [(sFy, 16)])
        W("sync", sFy, 16)
        flush()
        st.close()
        return nc

    with contextlib.ExitStack() as stL:
        ecL = stL.enter_context
        u2 = [ecL(nc.sbuf_tensor("u2_%d" % i, [P, D], F32)) for i in (0, 1)]
        o_t = [ecL(nc.sbuf_tensor("o_t%d" % i, [P, D], F32)) for i in (0, 1)]
        sq2 = ecL(nc.sbuf_tensor("sq2", [P, D], F32))
        sum2 = ecL(nc.sbuf_tensor("sum2", [P, 1], F32))
        nm2 = ecL(nc.sbuf_tensor("nm2", [P, 1], F32))
        ssq2 = ecL(nc.sbuf_tensor("ssq2", [P, 1], F32))
        lnv2 = ecL(nc.sbuf_tensor("lnv2", [P, 1], F32))
        rstd2 = ecL(nc.sbuf_tensor("rstd2", [P, 1], F32))

        sU2 = [alias("sXT"), alias("sX1r")]
        u2_fills = [0, 0]
        sS5 = alias("sPSG")
        sOUT = alias("sWO")
        out_done = [0, 0]

        W("sync", sACCY_out)
        for tt in range(NTT):
            slot = tt % 2
            if u2_fills[slot] > 0:
                W("sync", sS5, tt - 1)
            E("sync",
              lambda e, s=slot, tt=tt:
              e.dma_start(out=u2[s][:], in_=y_scr[tt * P:(tt + 1) * P, :]),
              [(sU2[slot], 16)])
            u2_fills[slot] += 1
            W("scalar", sU2[slot], 16 * u2_fills[slot])
            if tt == 0:
                W("scalar", sC, 16 * 8)
            if out_done[slot] > 0:
                W("scalar", sOUT, 16 * out_done[slot])
            E("scalar", lambda e, s=slot:
              e.activation(sq2[:, :], u2[s][:, :], AF.Identity,
                           accum_out=sum2[:, 0:1]))
            DR("scalar")
            E("scalar", lambda e: e.mul(nm2[:, 0:1], sum2[:, 0:1], -1.0 / D))
            DR("scalar")
            E("scalar", lambda e, s=slot:
              e.activation(sq2[:, :], u2[s][:, :], AF.Square, bias=nm2[:, 0:1],
                           accum_out=ssq2[:, 0:1]))
            DR("scalar")
            E("scalar", lambda e:
              e.activation(lnv2[:, 0:1], ssq2[:, 0:1], AF.Ln, scale=1.0 / D,
                           bias=epsc_sb[:, 0:1]))
            DR("scalar")
            E("scalar", lambda e:
              e.activation(rstd2[:, 0:1], lnv2[:, 0:1], AF.Exp, scale=-0.5))
            DR("scalar")
            E("scalar", lambda e, s=slot:
              e.activation(sq2[:, :], u2[s][:, :], AF.Identity, bias=nm2[:, 0:1]))
            DR("scalar")
            E("scalar", lambda e, s=slot:
              e.activation(o_t[s][:, :], sq2[:, :], AF.Identity,
                           scale=rstd2[:, 0:1]),
              [(sS5, 1)])
            W("sync", sS5, tt + 1)
            E("sync",
              lambda e, s=slot, tt=tt:
              e.dma_start(out=out[tt * P:(tt + 1) * P, :], in_=o_t[s][:, :]),
              [(sOUT, 16)])
            out_done[slot] = tt + 1
        W("sync", sOUT, 16 * NTT)
        flush()

    st.close()
    return nc



def _devfmt_w(wt):
    return np.ascontiguousarray(wt.reshape(16, P, 16, P).transpose(0, 3, 2, 1))


def kernel(L0_attn_x, L0_attn_Wq, L0_attn_bq, L0_attn_Wk, L0_attn_bk,
           L0_attn_Wv, L0_attn_bv, L0_attn_Wo, L0_attn_bo,
           L0_ffn_Wg, L0_ffn_W1, L0_ffn_W3, L0_ffn_W2,
           L0_ffn_Ws1, L0_ffn_Ws3, L0_ffn_Ws2):
    f32 = np.float32
    x = np.asarray(L0_attn_x, dtype=f32)
    Wq = np.asarray(L0_attn_Wq, f32); bq = np.asarray(L0_attn_bq, f32)
    Wk = np.asarray(L0_attn_Wk, f32); bk = np.asarray(L0_attn_bk, f32)
    Wv = np.asarray(L0_attn_Wv, f32); bv = np.asarray(L0_attn_bv, f32)
    Wo = np.asarray(L0_attn_Wo, f32); bo = np.asarray(L0_attn_bo, f32)
    Wg = np.asarray(L0_ffn_Wg, f32)
    W1 = np.asarray(L0_ffn_W1, f32); W3 = np.asarray(L0_ffn_W3, f32)
    W2 = np.asarray(L0_ffn_W2, f32)
    Ws1 = np.asarray(L0_ffn_Ws1, f32); Ws3 = np.asarray(L0_ffn_Ws3, f32)
    Ws2 = np.asarray(L0_ffn_Ws2, f32)

    wq_dev = _devfmt_w(Wq)
    wk_dev = _devfmt_w(Wk)
    wv_dev = _devfmt_w(Wv)
    wo_dev = np.ascontiguousarray(Wo.T.reshape(16, P, D).transpose(1, 0, 2))
    wg_dev = np.ascontiguousarray(Wg.T.reshape(16, P, 16).transpose(1, 0, 2))
    bq_l = np.ascontiguousarray(bq.reshape(16, P).T)
    bk_l = np.ascontiguousarray(bk.reshape(16, P).T)
    bv_l = np.ascontiguousarray(bv.reshape(16, P).T)
    bo_rep = np.ascontiguousarray(np.broadcast_to(bo, (P, D)))

    w1r = W1.reshape(16, NIC, P, 16, P).transpose(0, 1, 4, 3, 2)
    w1s = Ws1.reshape(2, NIC, P, 16, P).transpose(0, 1, 4, 3, 2)
    w1_dev = np.ascontiguousarray(np.concatenate([w1r, w1s], axis=0))
    w3r = W3.reshape(16, NIC, P, 16, P).transpose(0, 1, 4, 3, 2)
    w3s = Ws3.reshape(2, NIC, P, 16, P).transpose(0, 1, 4, 3, 2)
    w3_dev = np.ascontiguousarray(np.concatenate([w3r, w3s], axis=0))
    w2r = W2.reshape(16, NDC4, 512, NIC, P).transpose(0, 1, 4, 3, 2)
    w2s = Ws2.T.reshape(2, NIC, P, NDC4, 512).transpose(0, 3, 2, 1, 4)
    w2_dev = np.ascontiguousarray(np.concatenate([w2r, w2s], axis=0))

    ident = np.eye(P, dtype=f32)
    ones1 = np.ones((1, P), dtype=f32)
    epsc = np.full((P, 1), EPS, dtype=f32)
    zpre = np.zeros((P, NDCH, NT), dtype=f32)

    shared = dict(
        wq_dev=wq_dev, wk_dev=wk_dev, wv_dev=wv_dev, wo_dev=wo_dev,
        bq_l=bq_l, bk_l=bk_l, bv_l=bv_l, bo_rep=bo_rep, wg_dev=wg_dev,
        w1_dev=w1_dev, w3_dev=w3_dev, w2_dev=w2_dev,
        ident=ident, ones1=ones1, epsc=epsc,
    )

    in_maps = []
    for c in range(NCORES):
        b, hh = divmod(c, 2)
        sl = slice(hh * NT, (hh + 1) * NT)
        xtm = np.ascontiguousarray(x[b, sl, :])
        xdev = np.ascontiguousarray(xtm.T.reshape(16, P, NT).transpose(1, 0, 2))
        if hh == 1:
            xpre = np.ascontiguousarray(
                x[b, 0:NT, :].T.reshape(16, P, NT).transpose(1, 0, 2))
            cm = np.ones((P, 1), dtype=f32)
        else:
            xpre = zpre
            cm = np.zeros((P, 1), dtype=f32)
        m = dict(shared)
        m.update(xdev=xdev, xtm=xtm, xpre=xpre, cmask=cm)
        in_maps.append(m)

    nc = build_nc()
    res = run_bass_kernel_spmd(nc, in_maps, core_ids=list(range(NCORES)))

    full = np.empty((4, 4096, D), dtype=f32)
    for c in range(NCORES):
        b, hh = divmod(c, 2)
        full[b, hh * NT:(hh + 1) * NT, :] = res.results[c]["out"]
    return full


# version 03
# speedup vs baseline: 1.0315x; 1.0315x over previous
import contextlib
import numpy as np

_PLAN = None

import concourse.bass as bass
import concourse.mybir as mybir
from concourse.bass_utils import run_bass_kernel_spmd

F32 = mybir.dt.float32
BF16 = mybir.dt.bfloat16
AF = mybir.ActivationFunctionType
ALU = mybir.AluOpType
AX = mybir.AxisListType

P = 128
D = 2048
NDCH = 16
NT = 2048
NTCH = 4
NTT = 16
INTER = 1408
NIC = 11
NEXP = 18
NPAIR = 9
NHALF = 2
HALF = 1024
TT8 = 8
NTC2 = 2
NDC4 = 4
SCALE = float(128 ** -0.5)
EPS = 1e-5
NCORES = 8

ENGINES = ("sync", "gpsimd", "vector", "scalar", "tensor")

_STAGE = "full"
_SERIAL = False
_SERIALDMA = False


class Sem:
    __slots__ = ("h", "n", "base")

    def __init__(self, h):
        self.h = h
        self.n = 0
        self.base = 0


def build_nc():
    nc = bass.Bass("TRN2", target_bir_lowering=False)

    dt = {}

    def din(name, shape, dtype=F32):
        dt[name] = nc.dram_tensor(name, list(shape), dtype, kind="ExternalInput").ap()

    din("xdev", (P, NDCH, NT))
    din("xtm", (NT, D))
    din("xpre", (P, NDCH, NT))
    din("cmask", (P, 1))
    din("wq_dev", (16, P, NDCH, P))
    din("wk_dev", (16, P, NDCH, P))
    din("wv_dev", (16, P, NDCH, P))
    din("wo_dev", (P, NDCH, D))
    din("bq_l", (P, 16))
    din("bk_l", (P, 16))
    din("bv_l", (P, 16))
    din("bo_rep", (P, D))
    din("wg_dev", (P, NDCH, 16))
    din("w1_dev", (NEXP, NIC, P, NDCH, P))
    din("w3_dev", (NEXP, NIC, P, NDCH, P))
    din("w2_dev", (NEXP, NDC4, P, NIC, 512))
    din("ident", (P, P))
    din("ones1", (1, P))
    din("epsc", (P, 1))
    din("sentv", (P, 64))

    out = nc.dram_tensor("out", [NT, D], F32, kind="ExternalOutput").ap()
    sent = nc.dram_tensor("sent", [P, 64], F32, kind="ExternalOutput").ap()
    prog = nc.dram_tensor("prog", [64, 64], F32, kind="ExternalOutput").ap()
    x1fm_scr = nc.dram_tensor("x1fm_scr", [P, NDCH, NT], BF16, kind="Internal").ap()
    y_scr = nc.dram_tensor("y_scr", [NT, D], F32, kind="Internal").ap()
    comb_hbm = nc.dram_tensor("comb_hbm", [16, NT], F32, kind="Internal").ap()

    st = contextlib.ExitStack()
    ec = st.enter_context

    ident_sb = ec(nc.sbuf_tensor("ident_sb", [P, P], F32))
    bq_sb = ec(nc.sbuf_tensor("bq_sb", [P, 16], F32))
    bk_sb = ec(nc.sbuf_tensor("bk_sb", [P, 16], F32))
    bv_sb = ec(nc.sbuf_tensor("bv_sb", [P, 16], F32))
    cmask_sb = ec(nc.sbuf_tensor("cmask_sb", [P, 1], F32))
    wg_sb = ec(nc.sbuf_tensor("wg_sb", [P, NDCH, 16], F32))
    epsc_sb = ec(nc.sbuf_tensor("epsc_sb", [P, 1], F32))
    sent_sb = ec(nc.sbuf_tensor("sent_sb", [P, 64], F32))
    ones_sb = ec(nc.sbuf_tensor("ones_sb", [1, P], F32))
    carry0 = ec(nc.sbuf_tensor("carry0", [P, 16], F32))
    carry_eff = ec(nc.sbuf_tensor("carry_eff", [P, 16], F32))
    zc = ec(nc.sbuf_tensor("zc", [P, 1], F32))
    combT = ec(nc.sbuf_tensor("combT", [16, NT], F32))
    scores_all = ec(nc.sbuf_tensor("scores_all", [P, NTT, 16], F32))
    comb_all = ec(nc.sbuf_tensor("comb_all", [P, NTT, 16], F32))
    r_gbuf = ec(nc.sbuf_tensor("r_gbuf", [P, NTT, 4], F32))
    r_m1 = ec(nc.sbuf_tensor("r_m1", [P, NTT, 1], F32))
    r_m2 = ec(nc.sbuf_tensor("r_m2", [P, NTT, 1], F32))
    r_t1 = ec(nc.sbuf_tensor("r_t1", [P, NTT, 16], F32))
    r_t2 = ec(nc.sbuf_tensor("r_t2", [P, NTT, 16], F32))
    r_pen = ec(nc.sbuf_tensor("r_pen", [P, NTT, 4], F32))
    r_sum = ec(nc.sbuf_tensor("r_sum", [P, NTT, 1], F32))

    ps = [ec(nc.psum_tensor("ps%d" % i, [P, 512], F32)) for i in range(8)]

    blk = ec(nc.Block())

    steps = {k: [] for k in ENGINES}
    sems = {}
    global _PLAN
    _PLAN = {k: [] for k in ENGINES}

    def sem(name):
        if name not in sems:
            s = Sem(ec(nc.semaphore(name)))
            _PLAN.setdefault("_names", {})[id(s.h)] = name
            sems[name] = s
        return sems[name]

    def alias(old_name):
        phys = sems[old_name]
        s = Sem(phys.h)
        s.base = phys.n
        s.n = phys.n
        return s

    def E(eng, fn, incs=()):
        for s, k in incs:
            s.n += k

        def run(e, fn=fn, incs=tuple(incs)):
            instr = fn(e)
            for s, k in incs:
                instr.then_inc(s.h, k)

        steps[eng].append(run)
        _PLAN[eng].append(("op", tuple((id(s.h), k) for s, k in incs)))

    def W(eng, s, val=None):
        v = s.n if val is None else s.base + val
        if v > 0:
            steps[eng].append(lambda e, h=s.h, v=v: e.wait_ge(h, v))
            _PLAN[eng].append(("wait", id(s.h), v))

    def DR(eng):
        steps[eng].append(lambda e: e.drain())

    def flush():
        for k in ENGINES:
            fns = steps[k]
            if not fns:
                continue
            deco = getattr(blk, k)

            def runner(e, fns=fns):
                for f in fns:
                    f(e)

            deco(runner)
            steps[k] = []

    sC = sem("sC")
    for name, tile in (
        ("ident", ident_sb), ("bq_l", bq_sb), ("bk_l", bk_sb), ("bv_l", bv_sb),
        ("cmask", cmask_sb), ("wg_dev", wg_sb), ("epsc", epsc_sb), ("ones1", ones_sb),
    ):
        E("sync", lambda e, t=tile, n=name: e.dma_start(out=t[:], in_=dt[n]), [(sC, 16)])
    E("vector", lambda e: e.memset(zc[:], 0.0))
    E("sync", lambda e: e.dma_start(out=sent_sb[:], in_=dt["sentv"]), [(sC, 16)])

    mk_n = [0]
    sMRK = None

    def MARK(eng="sync"):
        nonlocal sMRK
        if sMRK is None:
            sMRK = sem("sMRK")
        k = mk_n[0]
        mk_n[0] += 1
        W(eng, sem("sC"), 16 * 9)
        E(eng, lambda e, k=k: e.dma_start(out=prog[k:k + 1, :], in_=sent_sb[0:1, :]),
          [(sMRK, 16)])

    sVATT = sem("sVATT")
    sP3V = sem("sP3V")

    stCtx = contextlib.ExitStack()
    ctx_b = stCtx.enter_context(nc.sbuf_tensor("ctx_b", [P, NDCH, NT], BF16))

    stXA = contextlib.ExitStack()
    x_arena = stXA.enter_context(nc.sbuf_tensor("x_arena", [P, NDCH, NT], BF16))
    with contextlib.ExitStack() as stA:
        ecA = stA.enter_context
        wslot = {(pr, s): ecA(nc.sbuf_tensor("w%s%d" % (pr, s), [P, NDCH, P], BF16))
                 for pr in ("q", "k", "v") for s in (0, 1)}
        pkv_e = ecA(nc.sbuf_tensor("pkv_e", [P, NT], F32))
        phiq_e = ecA(nc.sbuf_tensor("phiq_e", [P, NT], F32))
        mt_s = {(pr, s): ecA(nc.sbuf_tensor("mt%s%d" % (pr, s), [P, 512], F32))
                for pr in ("q", "k") for s in (0, 1)}
        ex_s = {(pr, s): ecA(nc.sbuf_tensor("ex%s%d" % (pr, s), [P, 512], F32))
                for pr in ("q", "k") for s in (0, 1)}
        vp_s = ecA(nc.sbuf_tensor("vp_s", [P, 512], F32))

        sXA = sem("sXA")
        sPFX = sem("sPFX")
        sVT = sem("sVT")
        sWR = {k: sem("sWR_%s%d" % k) for k in wslot}
        wfills = {k: 0 for k in wslot}
        w_rel_at = {k: None for k in wslot}
        bank = {("q", 0): 0, ("q", 1): 3, ("k", 0): 1, ("k", 1): 4,
                ("v", 0): 2, ("v", 1): 5}
        sBR = {k: sem("sBR_%s%d" % k) for k in bank}
        sBF = {k: sem("sBF_%s%d" % k) for k in bank}
        bfills = {k: 0 for k in bank}
        sMTr = {k: sem("sMTr_%s%d" % k) for k in mt_s}
        mtfills = {k: 0 for k in mt_s}
        sEXr = {k: sem("sEXr_%s%d" % k) for k in ex_s}
        sEXf = {k: sem("sEXf_%s%d" % k) for k in ex_s}
        exfills = {k: 0 for k in ex_s}

        def load_w(pr, ech, src):
            s = ech % 2
            k = (pr, s)
            if _SERIALDMA:
                W("gpsimd", sVT)
            if w_rel_at[k] is not None:
                W("gpsimd", w_rel_at[k][0], w_rel_at[k][1])
            E("gpsimd",
              lambda e, t=wslot[k], n=src, ech=ech: e.dma_start(out=t[:], in_=dt[n][ech]),
              [(sWR[k], 16)])
            wfills[k] += 1

        def proj_mms(pr, ech, g, tch, release_w):
            s = ech % 2
            par = g % 2
            bnk = bank[(pr, par)]
            W("tensor", sWR[(pr, s)], 16 * wfills[(pr, s)])
            if bfills[(pr, par)] > 0:
                W("tensor", sBF[(pr, par)], bfills[(pr, par)])
            for dch in range(NDCH):
                E("tensor",
                  lambda e, bnk=bnk, w=wslot[(pr, s)], dch=dch, tch=tch:
                  e.matmul(ps[bnk][:, :], w[:, dch, :],
                           x_arena[:, dch, tch * 512:(tch + 1) * 512],
                           start=(dch == 0), stop=(dch == NDCH - 1)),
                  [(sBR[(pr, par)], 1)] if dch == NDCH - 1 else ())
            bfills[(pr, par)] += 1
            if release_w:
                w_rel_at[(pr, s)] = (sBF[(pr, par)], bfills[(pr, par)])

        def phi_chain(pr, ech, g, bias_sb, dest):
            par = g % 2
            bnk = bank[(pr, par)]
            mk = (pr, par)
            W("vector", sBR[(pr, par)], bfills[(pr, par)])
            if mtfills[mk] > 0:
                W("vector", sEXr[mk], mtfills[mk])
            E("vector",
              lambda e, bnk=bnk, mk=mk, b=bias_sb, ech=ech:
              e.tensor_scalar(mt_s[mk][:, :], ps[bnk][:, :], b[:, ech:ech + 1], 0.0,
                              ALU.add, ALU.min),
              [(sMTr[mk], 1)])
            mtfills[mk] += 1
            W("scalar", sMTr[mk], mtfills[mk])
            if exfills[mk] > 0:
                W("scalar", sEXf[mk], exfills[mk])
            E("scalar",
              lambda e, mk=mk: e.activation(ex_s[mk][:, :], mt_s[mk][:, :], AF.Exp),
              [(sEXr[mk], 1)])
            exfills[mk] += 1
            E("vector",
              lambda e, bnk=bnk, b=bias_sb, ech=ech, d=dest:
              e.tensor_scalar(d, ps[bnk][:, :], b[:, ech:ech + 1], 0.0, ALU.add, ALU.max),
              [(sBF[(pr, par)], 1)])
            W("vector", sEXr[mk], exfills[mk])
            E("vector",
              lambda e, mk=mk, d=dest: e.tensor_tensor(d, d, ex_s[mk][:, :], ALU.add),
              [(sEXf[mk], 1)])

        def v_evac(ech, g):
            par = g % 2
            bnk = bank[("v", par)]
            W("vector", sBR[("v", par)], bfills[("v", par)])
            E("vector",
              lambda e, bnk=bnk, ech=ech:
              e.tensor_scalar(vp_s[:, :], ps[bnk][:, :], bv_sb[:, ech:ech + 1],
                              None, ALU.add),
              [(sBF[("v", par)], 1)])

        E("gpsimd", lambda e: e.dma_start(out=x_arena[:], in_=dt["xpre"]), [(sXA, 16)])
        load_w("k", 0, "wk_dev")
        load_w("v", 0, "wv_dev")
        g = 0
        for ech in range(16):
            if ech + 1 < 16:
                load_w("k", ech + 1, "wk_dev")
                load_w("v", ech + 1, "wv_dev")
            if ech == 0:
                W("tensor", sXA, 16)
                W("vector", sC, 16 * 9)
            for tch in range(NTCH):
                rel = (tch == NTCH - 1)
                if _SERIAL:
                    W("tensor", sVT)
                proj_mms("k", ech, g, tch, rel)
                proj_mms("v", ech, g, tch, rel)
                dest = pkv_e[:, tch * 512:(tch + 1) * 512]
                phi_chain("k", ech, g, bk_sb, dest)
                v_evac(ech, g)
                E("vector", lambda e, d=dest: e.tensor_tensor(d, d, vp_s[:, :], ALU.mult),
                  [(sVT, 1)])
                g += 1
            E("vector",
              lambda e, ech=ech:
              e.reduce_sum(carry0[:, ech:ech + 1], pkv_e[:, :], axis=AX.X))
        DR("vector")
        E("vector",
          lambda e: e.tensor_scalar(carry_eff[:, :], carry0[:, :], cmask_sb[:, 0:1],
                                    None, ALU.mult),
          [(sPFX, 1)])
        DR("vector")

        W("gpsimd", sPFX, 1)
        E("gpsimd", lambda e: e.dma_start(out=x_arena[:], in_=dt["xdev"]), [(sXA, 16)])
        load_w("q", 0, "wq_dev")
        load_w("k", 0, "wk_dev")
        load_w("v", 0, "wv_dev")
        for ech in range(16):
            if ech + 1 < 16:
                load_w("q", ech + 1, "wq_dev")
                load_w("k", ech + 1, "wk_dev")
                load_w("v", ech + 1, "wv_dev")
            if ech == 0:
                W("tensor", sXA, 32)
            for tch in range(NTCH):
                rel = (tch == NTCH - 1)
                if _SERIAL:
                    W("tensor", sVT)
                proj_mms("q", ech, g, tch, rel)
                proj_mms("k", ech, g, tch, rel)
                proj_mms("v", ech, g, tch, rel)
                phi_chain("q", ech, g, bq_sb, phiq_e[:, tch * 512:(tch + 1) * 512])
                dest = pkv_e[:, tch * 512:(tch + 1) * 512]
                phi_chain("k", ech, g, bk_sb, dest)
                v_evac(ech, g)
                E("vector", lambda e, d=dest: e.tensor_tensor(d, d, vp_s[:, :], ALU.mult),
                  [(sVT, 1)])
                g += 1
            E("vector",
              lambda e, ech=ech:
              e.tensor_tensor_scan(pkv_e[:, :], pkv_e[:, :],
                                   zc[:, 0:1].broadcast_to([P, NT]),
                                   carry_eff[:, ech:ech + 1], ALU.add, ALU.add))
            DR("vector")
            E("vector",
              lambda e, ech=ech:
              e.scalar_tensor_tensor(ctx_b[:, ech, :], phiq_e[:, :], SCALE, pkv_e[:, :],
                                     ALU.mult, ALU.mult),
              [(sVATT, 1)])

        if _STAGE == "ctx":
            sF = sem("sF")
            W("gpsimd", sVATT, 16)
            E("gpsimd",
              lambda e: e.dma_start(out=out.rearrange("(c p) t -> p c t", p=P),
                                    in_=ctx_b[:]), [(sF, 16)])
            W("sync", sF, 16)
            E("sync", lambda e: e.dma_start(out=sent, in_=sent_sb[:]), [(sem("sFs"), 16)])
            W("sync", sem("sFs"), 16)
        bank_sems_final = [(sBR[k], sBR[k].n) for k in bank]
        pass
    stXA.close()
    if _STAGE == "ctx":
        flush()
        stCtx.close()
        st.close()
        return nc

    with contextlib.ExitStack() as stB:
        ecB = stB.enter_context
        woT_b = ecB(nc.sbuf_tensor("woT_b", [P, NDCH, D], BF16))
        borep_sb = ecB(nc.sbuf_tensor("borep_sb", [P, D], F32))
        u_t = ecB(nc.sbuf_tensor("u_t", [P, D], F32))
        xbo = ecB(nc.sbuf_tensor("xbo", [P, D], F32))
        xtm_t = ecB(nc.sbuf_tensor("xtm_t", [P, D], F32))
        x1tm_t = ecB(nc.sbuf_tensor("x1tm_t", [P, D], F32))
        x1f32 = ecB(nc.sbuf_tensor("x1f32", [P, NDCH, P], F32))
        x1b16 = ecB(nc.sbuf_tensor("x1b16", [P, NDCH, P], BF16))
        msum = ecB(nc.sbuf_tensor("msum", [P, 4], F32))
        mred = ecB(nc.sbuf_tensor("mred", [P, 1], F32))
        meanb = ecB(nc.sbuf_tensor("meanb", [P, 1], F32))
        m2b = ecB(nc.sbuf_tensor("m2b", [P, 1], F32))
        ssqb = ecB(nc.sbuf_tensor("ssqb", [P, 1], F32))
        varb = ecB(nc.sbuf_tensor("varb", [P, 1], F32))
        lnvb = ecB(nc.sbuf_tensor("lnvb", [P, 1], F32))
        rstdb = ecB(nc.sbuf_tensor("rstdb", [P, 1], F32))

        sWO = sem("sWO")
        sBOR = sem("sBOR")
        sXT = sem("sXT")
        sXTf = sem("sXTf")
        xt_fills = 0
        sX1r = sem("sX1r")
        sX1f = sem("sX1f")
        x1_fills = 0
        sPSA = [sem("sPSA%d" % i) for i in range(4)]
        sPSAf = [sem("sPSAf%d" % i) for i in range(4)]
        psa_fills = [0, 0, 0, 0]
        psa_fill_at = {}
        sTRr = [sem("sTRr%d" % i) for i in (0, 1)]
        sTRE = sem("sTRE")
        tr_fills = [0, 0]
        tr_rel_at = {0: 0, 1: 0}
        sPEG = sem("sPEG")
        sPSG = sem("sPSG")
        sPSGf = sem("sPSGf")
        sVLN = sem("sVLN")
        sRSTD = sem("sRSTD")
        sYW = sem("sYW")
        sXW = sem("sXW")
        sX1B = sem("sX1B")

        W("gpsimd", sVATT, 16)
        for cpart in range(4):
            E("gpsimd",
              lambda e, c=cpart:
              e.dma_start(out=woT_b[:, c * 4:(c + 1) * 4, :],
                          in_=dt["wo_dev"][:, c * 4:(c + 1) * 4, :]),
              [(sWO, 16)])
        W("sync", sVATT, 16)
        MARK()
        E("sync", lambda e: e.dma_start(out=borep_sb[:], in_=dt["bo_rep"]), [(sBOR, 16)])
        E("sync", lambda e: e.dma_start(out=xtm_t[:], in_=dt["xtm"][0:P, :]), [(sXT, 16)])
        xt_fills += 1

        def emit_wo(tt):
            if tt == 0:
                W("tensor", sWO, 64)
                W("tensor", sVATT, 16)
            for dc in range(NDC4):
                par = (tt * NDC4 + dc) % 4
                if psa_fills[par] > 0:
                    W("tensor", sPSAf[par], psa_fills[par])
                for dch in range(NDCH):
                    E("tensor",
                      lambda e, par=par, dch=dch, tt=tt, dc=dc:
                      e.matmul(ps[par][:, :], ctx_b[:, dch, tt * P:(tt + 1) * P],
                               woT_b[:, dch, dc * 512:(dc + 1) * 512],
                               start=(dch == 0), stop=(dch == NDCH - 1)),
                      [(sPSA[par], 1)] if dch == NDCH - 1 else ())
                psa_fills[par] += 1
                psa_fill_at[(tt, dc)] = psa_fills[par]

        def emit_ln1(tt):
            nonlocal xt_fills, x1_fills
            W("vector", sXT, 16 * xt_fills)
            if tt == 0:
                W("vector", sBOR, 16)
            E("vector",
              lambda e: e.tensor_tensor(xbo[:, :], xtm_t[:, :], borep_sb[:, :], ALU.add),
              [(sXTf, 1)])
            if tt + 1 < NTT:
                W("sync", sXTf, xt_fills)
                E("sync",
                  lambda e, tt=tt:
                  e.dma_start(out=xtm_t[:], in_=dt["xtm"][(tt + 1) * P:(tt + 2) * P, :]),
                  [(sXT, 16)])
                xt_fills += 1
            for dc in range(NDC4):
                par = (tt * NDC4 + dc) % 4
                W("vector", sPSA[par], psa_fill_at[(tt, dc)])
                E("vector",
                  lambda e, par=par, dc=dc:
                  e.scalar_tensor_tensor(u_t[:, dc * 512:(dc + 1) * 512], ps[par][:, :],
                                         0.0, xbo[:, dc * 512:(dc + 1) * 512],
                                         ALU.add, ALU.add, accum_out=msum[:, dc:dc + 1]),
                  [(sPSAf[par], 1)])
            DR("vector")
            E("vector", lambda e: e.reduce_sum(mred[:, 0:1], msum[:, :], axis=AX.X))
            DR("vector")
            E("vector", lambda e: e.tensor_scalar(meanb[:, 0:1], mred[:, 0:1],
                                                  1.0 / D, None, ALU.mult))
            E("vector",
              lambda e: e.scalar_tensor_tensor(xbo[:, :], u_t[:, :], 1.0, u_t[:, :],
                                               ALU.mult, ALU.mult,
                                               accum_out=ssqb[:, 0:1]))
            DR("vector")
            E("vector", lambda e: e.tensor_tensor(m2b[:, 0:1], meanb[:, 0:1],
                                                  meanb[:, 0:1], ALU.mult))
            E("vector",
              lambda e: e.scalar_tensor_tensor(varb[:, 0:1], ssqb[:, 0:1], 1.0 / D,
                                               m2b[:, 0:1], ALU.mult, ALU.subtract))
            E("vector",
              lambda e: e.scalar_tensor_tensor(varb[:, 0:1], ssqb[:, 0:1], 1.0 / D,
                                               m2b[:, 0:1], ALU.mult, ALU.subtract),
              [(sVLN, 1)])
            W("scalar", sVLN)
            E("scalar", lambda e: e.activation(lnvb[:, 0:1], varb[:, 0:1], AF.Ln,
                                               bias=epsc_sb[:, 0:1]))
            DR("scalar")
            E("scalar", lambda e: e.activation(rstdb[:, 0:1], lnvb[:, 0:1], AF.Exp,
                                               scale=-0.5))
            E("scalar", lambda e: e.activation(rstdb[:, 0:1], lnvb[:, 0:1], AF.Exp,
                                               scale=-0.5), [(sRSTD, 1)])
            if x1_fills > 0:
                W("vector", sYW, 16 * x1_fills)
            W("vector", sRSTD)
            E("vector",
              lambda e:
              e.tensor_scalar(x1tm_t[:, :], u_t[:, :], meanb[:, 0:1], rstdb[:, 0:1],
                              ALU.subtract, ALU.mult),
              [(sX1r, 1)])
            x1_fills += 1
            W("sync", sX1r, x1_fills)
            MARK()
            E("sync",
              lambda e, tt=tt:
              e.dma_start(out=y_scr[tt * P:(tt + 1) * P, :], in_=x1tm_t[:, :]),
              [(sYW, 16)])

        def emit_tr_gate(tt):
            W("tensor", sX1r, x1_fills)
            if tt > 0:
                W("tensor", sPSGf, tt)
            for grp in range(4):
                tp = grp % 2
                if tr_fills[tp] > 0:
                    W("tensor", sTRE, tr_rel_at[tp])
                for j in range(4):
                    dch = grp * 4 + j
                    E("tensor",
                      lambda e, tp=tp, j=j, dch=dch:
                      e.transpose(ps[4 + tp][:, j * P:(j + 1) * P],
                                  x1tm_t[:, dch * P:(dch + 1) * P], ident_sb[:]),
                      [(sTRr[tp], 1)] if j == 3 else ())
                tr_fills[tp] += 1
                W("vector", sTRr[tp], tr_fills[tp])
                E("vector",
                  lambda e, tp=tp, grp=grp:
                  e.tensor_copy(x1f32[:, grp * 4:(grp + 1) * 4, :], ps[4 + tp][:, :]),
                  [(sTRE, 1)])
                tr_rel_at[tp] = sTRE.n
            if tt > 0:
                W("vector", sXW, 16 * tt)
            E("vector", lambda e: e.tensor_copy(x1b16[:, :, :], x1f32[:, :, :]),
              [(sX1B, 1)])
            W("sync", sX1B)
            E("sync",
              lambda e, tt=tt:
              e.dma_start(out=x1fm_scr[:, :, tt * P:(tt + 1) * P], in_=x1b16[:, :, :]),
              [(sXW, 16)])
            W("tensor", sTRE)
            for dch in range(NDCH):
                E("tensor",
                  lambda e, dch=dch:
                  e.matmul(ps[7][:, 0:16], x1f32[:, dch, :], wg_sb[:, dch, :],
                           start=(dch == 0), stop=(dch == NDCH - 1)),
                  [(sPSG, 1)] if dch == NDCH - 1 else ())
            W("vector", sPSG)
            E("vector",
              lambda e, tt=tt: e.tensor_copy(scores_all[:, tt, :], ps[7][:, 0:16]),
              [(sPSGf, 1)])

        emit_wo(0)
        for tt in range(NTT):
            if tt + 1 < NTT:
                emit_wo(tt + 1)
            emit_ln1(tt)
            emit_tr_gate(tt)

        sEXPR = sem("sEXPR")
        sEXPRf = sem("sEXPRf")
        sROUT = sem("sROUT")
        s4 = scores_all[:, :, :].rearrange("p t (g f) -> p t g f", f=4)
        t14 = r_t1[:, :, :].rearrange("p t (g f) -> p t g f", f=4)
        t24 = r_t2[:, :, :].rearrange("p t (g f) -> p t g f", f=4)

        def RV(fn, incs=()):
            E("vector", fn, incs)

        RV(lambda e: e.reduce_max(r_m1[:, :, 0:1], scores_all[:, :, :], axis=AX.X))
        DR("vector")
        RV(lambda e: e.tensor_scalar(r_m1[:, :, 0:1], r_m1[:, :, 0:1], -1.0, None, ALU.mult))
        RV(lambda e: e.tensor_tensor(r_t1[:, :, :], scores_all[:, :, :],
                                     r_m1[:, :, 0:1].broadcast_to([P, NTT, 16]), ALU.add),
           [(sEXPR, 1)])
        W("scalar", sEXPR)
        E("scalar",
          lambda e: e.activation(r_t2[:, :, :].rearrange("p t e -> p (t e)"),
                                 r_t1[:, :, :].rearrange("p t e -> p (t e)"), AF.Exp),
          [(sEXPRf, 1)])
        W("vector", sEXPRf)
        RV(lambda e: e.reduce_sum(r_sum[:, :, 0:1], r_t2[:, :, :], axis=AX.X))
        DR("vector")
        RV(lambda e: e.reciprocal(r_sum[:, :, 0:1].rearrange("p t o -> p (t o)"),
                                  r_sum[:, :, 0:1].rearrange("p t o -> p (t o)")))
        DR("vector")
        RV(lambda e: e.tensor_tensor(scores_all[:, :, :], r_t2[:, :, :],
                                     r_sum[:, :, 0:1].broadcast_to([P, NTT, 16]), ALU.mult))
        RV(lambda e: e.tensor_tensor(r_gbuf[:, :, :], s4[:, :, :, 0], s4[:, :, :, 1], ALU.max))
        RV(lambda e: e.tensor_tensor(r_pen[:, :, :], s4[:, :, :, 2], s4[:, :, :, 3], ALU.max))
        RV(lambda e: e.tensor_tensor(r_gbuf[:, :, :], r_gbuf[:, :, :], r_pen[:, :, :], ALU.max))
        RV(lambda e: e.reduce_max(r_m1[:, :, 0:1], r_gbuf[:, :, :], axis=AX.X))
        DR("vector")
        RV(lambda e: e.tensor_tensor(r_pen[:, :, :], r_gbuf[:, :, :],
                                     r_m1[:, :, 0:1].broadcast_to([P, NTT, 4]), ALU.is_ge))
        RV(lambda e: e.scalar_tensor_tensor(r_pen[:, :, :], r_pen[:, :, :], -1e30,
                                            r_gbuf[:, :, :], ALU.mult, ALU.add))
        RV(lambda e: e.reduce_max(r_m2[:, :, 0:1], r_pen[:, :, :], axis=AX.X))
        DR("vector")
        RV(lambda e: e.tensor_tensor(r_gbuf[:, :, :], r_gbuf[:, :, :],
                                     r_m2[:, :, 0:1].broadcast_to([P, NTT, 4]), ALU.is_ge))
        RV(lambda e: e.tensor_scalar(r_pen[:, :, :], r_gbuf[:, :, :], 1.0, 1e30,
                                     ALU.subtract, ALU.mult))
        for f in range(4):
            RV(lambda e, f=f: e.tensor_tensor(t14[:, :, :, f], s4[:, :, :, f],
                                              r_pen[:, :, :], ALU.add))
        for _ in range(3):
            RV(lambda e: e.reduce_max(r_m1[:, :, 0:1], r_t1[:, :, :], axis=AX.X))
            DR("vector")
            RV(lambda e: e.tensor_tensor(r_t2[:, :, :], r_t1[:, :, :],
                                         r_m1[:, :, 0:1].broadcast_to([P, NTT, 16]),
                                         ALU.is_ge))
            RV(lambda e: e.scalar_tensor_tensor(r_t1[:, :, :], r_t2[:, :, :], -1e30,
                                                r_t1[:, :, :], ALU.mult, ALU.add))
        RV(lambda e: e.reduce_max(r_m2[:, :, 0:1], r_t1[:, :, :], axis=AX.X))
        DR("vector")
        for f in range(4):
            RV(lambda e, f=f: e.tensor_tensor(t24[:, :, :, f], s4[:, :, :, f],
                                              r_pen[:, :, :], ALU.add))
        RV(lambda e: e.tensor_tensor(r_t2[:, :, :], r_t2[:, :, :],
                                     r_m2[:, :, 0:1].broadcast_to([P, NTT, 16]), ALU.is_ge))
        RV(lambda e: e.tensor_tensor(comb_all[:, :, :], scores_all[:, :, :],
                                     r_t2[:, :, :], ALU.mult),
           [(sROUT, 1)])

        sCTr = [sem("sCTr%d" % i) for i in (0, 1)]
        ct_fills = [0, 0]
        ct_rel_at = {0: 0, 1: 0}
        sCBW = sem("sCBW")
        sCB = sem("sCB")
        W("tensor", sROUT)
        for tt in range(NTT):
            par = tt % 2
            if ct_fills[par] > 0:
                W("tensor", sCBW, ct_rel_at[par])
            E("tensor",
              lambda e, par=par, tt=tt:
              e.transpose(ps[4 + par][0:16, 0:P], comb_all[:, tt, :], ident_sb[:]),
              [(sCTr[par], 1)])
            ct_fills[par] += 1
            W("vector", sCTr[par], ct_fills[par])
            E("vector",
              lambda e, par=par, tt=tt:
              e.tensor_copy(combT[0:16, tt * P:(tt + 1) * P], ps[4 + par][0:16, 0:P]),
              [(sCBW, 1)])
            ct_rel_at[par] = sCBW.n
        W("sync", sCBW, NTT)
        MARK()
        E("sync", lambda e: e.dma_start(out=comb_hbm, in_=combT[:, :]), [(sCB, 16)])
        E("vector", lambda e: e.engine_nop(), [(sP3V, 1)])

        if _STAGE in ("x1", "comb"):
            sF = sem("sF")
            if _STAGE == "x1":
                W("sync", sYW, 16 * NTT)
                E("sync", lambda e: e.dma_start(out=out, in_=y_scr), [(sF, 16)])
                E("sync", lambda e: e.dma_start(out=sent, in_=sent_sb[:]), [(sF, 16)])
            else:
                W("sync", sem("sCB"), 16)
                E("sync",
                  lambda e: e.dma_start(
                      out=out[0:P, 0:NTT * 16].rearrange("p (t e) -> p t e", e=16),
                      in_=comb_all[:, :, :]), [(sF, 16)])
            W("sync", sF, 32 if _STAGE == "x1" else 16)
        pass
    stCtx.close()
    if _STAGE in ("x1", "comb"):
        flush()
        st.close()
        return nc

    sACCY = None
    with contextlib.ExitStack() as stM:
        ecM = stM.enter_context
        x1h_b = ecM(nc.sbuf_tensor("x1h_b", [P, NDCH, HALF], BF16))
        h_pair = ecM(nc.sbuf_tensor("h_pair", [P, 2, NIC, HALF], BF16))
        w13 = {(w, s): ecM(nc.sbuf_tensor("w%d_%d" % (w, s), [P, NDCH, P], BF16))
               for w in (1, 3) for s in (0, 1)}
        w2c = [ecM(nc.sbuf_tensor("w2c%d" % s, [P, NIC, 512], BF16)) for s in range(3)]
        c_rep = ecM(nc.sbuf_tensor("c_rep", [P, 2, HALF], F32))
        sa_s = [ecM(nc.sbuf_tensor("sa_s%d" % i, [P, 512], F32)) for i in (0, 1)]
        cr_s = [ecM(nc.sbuf_tensor("cr_s%d" % i, [1, 2 * NT], F32)) for i in (0, 1)]
        bp_s = ecM(nc.sbuf_tensor("bp_s", [P, 512], F32))
        yev = [ecM(nc.sbuf_tensor("yev%d" % i, [P, 512], F32)) for i in (0, 1)]

        sX1H = alias("sXA")
        sW13r = {(1, 0): alias("sWR_q0"), (1, 1): alias("sWR_q1"),
                 (3, 0): alias("sWR_k0"), (3, 1): alias("sWR_k1")}
        w13_fills = {k: 0 for k in w13}
        w13_rel_at = [None, None]
        sW2r = [alias("sWR_v0"), alias("sWR_v1"), alias("sBR_q0")]
        w2_fills = [0, 0, 0]
        w2_rel_at = [None, None, None]
        sPAr = [alias("sBR_k0"), alias("sBR_k1")]
        pa_fills = [0, 0]
        sPBr = [alias("sBR_v0"), alias("sBR_v1")]
        sPBf = [alias("sBF_k0"), alias("sBF_k1")]
        pb_fills = [0, 0]
        pb_rel_at = [None, None]
        sPYr = [alias("sBF_v0"), alias("sBF_v1")]
        py_fills = [0, 0]
        sPCr = alias("sEXr_q0")
        sPCf = alias("sEXr_q1")
        pc_fills = 0
        sCRr = [alias("sEXr_k0"), alias("sEXr_k1")]
        cr_fills = [0, 0]
        cr_rel_at = [None, None]
        sSAr = [alias("sMTr_q0"), alias("sMTr_q1")]
        sa_fills = [0, 0]
        sa_rel_at = [0, 0]
        sYEr = [alias("sMTr_k0"), alias("sMTr_k1")]
        ye_fills = [0, 0]
        ye_rel_at = [0, 0]
        sHV = alias("sEXf_k0")
        sACCY = alias("sEXf_k1")

        chunk_seq = []
        for half in range(NHALF):
            for pair in range(NPAIR):
                for e01 in (0, 1):
                    eidx = 2 * pair + e01
                    for ic in range(NIC):
                        chunk_seq.append((eidx, ic))
        n_chunks = len(chunk_seq)

        def load_w13(j, first=False):
            eidx, ic = chunk_seq[j]
            slot = j % 2
            if first:
                W("gpsimd", sP3V, 1)
                W("gpsimd", sXW, 16 * NTT)
                W("gpsimd", sYW, 16 * NTT)
            elif w13_rel_at[slot] is not None:
                W("gpsimd", w13_rel_at[slot][0], w13_rel_at[slot][1])
            for w in (1, 3):
                k = (w, slot)
                E("gpsimd",
                  lambda e, k=k, w=w, eidx=eidx, ic=ic:
                  e.dma_start(out=w13[k][:], in_=dt["w%d_dev" % w][eidx, ic]),
                  [(sW13r[k], 16)])
                w13_fills[k] += 1

        w2_next = 0

        def load_w2(eidx, dch):
            nonlocal w2_next
            s = w2_next % 3
            w2_next += 1
            if w2_rel_at[s] is not None:
                W("gpsimd", w2_rel_at[s][0], w2_rel_at[s][1])
            E("gpsimd",
              lambda e, s=s, eidx=eidx, dch=dch:
              e.dma_start(out=w2c[s][:], in_=dt["w2_dev"][eidx, dch]),
              [(sW2r[s], 16)])
            w2_fills[s] += 1
            return s

        load_w13(0, first=True)
        ab_group = 0
        y_group = 0
        jchunk = 0

        for half in range(NHALF):
            if half == 0:
                W("sync", sXW, 16 * NTT)
                W("sync", sYW, 16 * NTT)
                W("sync", sP3V, 1)
            else:
                W("sync", sHV, sHV.n - sHV.base)
            E("sync",
              lambda e, half=half:
              e.dma_start(out=x1h_b[:],
                          in_=x1fm_scr[:, :, half * HALF:(half + 1) * HALF]),
              [(sX1H, 16)])
            if half == 0:
                W("gpsimd", sYW, 16 * NTT)

            for pair in range(NPAIR):
                shared = (pair == NPAIR - 1)
                e0, e1 = 2 * pair, 2 * pair + 1
                if not shared:
                    crslot = (half * NPAIR + pair) % 2
                    if cr_rel_at[crslot] is not None:
                        W("sync", cr_rel_at[crslot][0], cr_rel_at[crslot][1])
                    if half == 0 and pair == 0:
                        W("sync", sCB, 16)
                    for e01, eidx in ((0, e0), (1, e1)):
                        E("sync",
                          lambda e, s=crslot, e01=e01, eidx=eidx:
                          e.dma_start(out=cr_s[s][0:1, e01 * NT:(e01 + 1) * NT],
                                      in_=comb_hbm[eidx:eidx + 1, :]),
                          [(sCRr[crslot], 16)])
                    cr_fills[crslot] += 1
                    W("tensor", sCRr[crslot], 32 * cr_fills[crslot])
                    nmm = 0
                    for e01 in (0, 1):
                        for tc in range(NTC2):
                            if pc_fills > 0:
                                W("tensor", sPCf, pc_fills)
                            off = e01 * NT + half * HALF + tc * 512
                            nmm += 1
                            E("tensor",
                              lambda e, s=crslot, off=off:
                              e.matmul(ps[6][:, :], ones_sb[0:1, :],
                                       cr_s[s][0:1, off:off + 512],
                                       start=True, stop=True),
                              [(sPCr, 1)])
                            pc_fills += 1
                            W("vector", sPCr, pc_fills)
                            E("vector",
                              lambda e, e01=e01, tc=tc:
                              e.tensor_copy(c_rep[:, e01, tc * 512:(tc + 1) * 512],
                                            ps[6][:, :]),
                              [(sPCf, 1)])
                            if nmm == 4:
                                cr_rel_at[crslot] = (sPCf, sPCf.n - sPCf.base)
                for e01 in (0, 1):
                    for ic in range(NIC):
                        slot = jchunk % 2
                        if jchunk + 1 < n_chunks:
                            load_w13(jchunk + 1)
                        k1 = (1, slot)
                        k3 = (3, slot)
                        for tc in range(NTC2):
                            par = ab_group % 2
                            W("tensor", sW13r[k1], 16 * w13_fills[k1])
                            if ab_group == 0:
                                W("tensor", sX1H, 16)
                            if half == 1 and pair == 0 and e01 == 0 and ic == 0 and tc == 0:
                                W("tensor", sX1H, 32)
                            if pa_fills[par] > 0:
                                W("tensor", sSAr[par], pa_fills[par])
                            for dch in range(NDCH):
                                E("tensor",
                                  lambda e, par=par, k1=k1, dch=dch, tc=tc:
                                  e.matmul(ps[0 + 2 * par][:, :], w13[k1][:, dch, :],
                                           x1h_b[:, dch, tc * 512:(tc + 1) * 512],
                                           start=(dch == 0), stop=(dch == NDCH - 1)),
                                  [(sPAr[par], 1)] if dch == NDCH - 1 else ())
                            pa_fills[par] += 1
                            W("tensor", sW13r[k3], 16 * w13_fills[k3])
                            if pb_fills[par] > 0 and pb_rel_at[par] is not None:
                                W("tensor", pb_rel_at[par][0], pb_rel_at[par][1])
                            for dch in range(NDCH):
                                E("tensor",
                                  lambda e, par=par, k3=k3, dch=dch, tc=tc:
                                  e.matmul(ps[1 + 2 * par][:, :], w13[k3][:, dch, :],
                                           x1h_b[:, dch, tc * 512:(tc + 1) * 512],
                                           start=(dch == 0), stop=(dch == NDCH - 1)),
                                  [(sPBr[par], 1)] if dch == NDCH - 1 else ())
                            pb_fills[par] += 1
                            sslot = ab_group % 2
                            W("scalar", sPAr[par], pa_fills[par])
                            if sa_fills[sslot] > 0:
                                W("scalar", sHV, sa_rel_at[sslot])
                            E("scalar",
                              lambda e, par=par, ss=sslot:
                              e.activation(sa_s[ss][:, :], ps[0 + 2 * par][:, :], AF.Silu),
                              [(sSAr[sslot], 1)])
                            sa_fills[sslot] += 1
                            hdst = h_pair[:, e01, ic, tc * 512:(tc + 1) * 512]
                            W("vector", sPBr[par], pb_fills[par])
                            if shared:
                                W("vector", sSAr[sslot], sa_fills[sslot])
                                E("vector",
                                  lambda e, par=par, ss=sslot, hd=hdst:
                                  e.tensor_tensor(hd, sa_s[ss][:, :],
                                                  ps[1 + 2 * par][:, :], ALU.mult),
                                  [(sHV, 1)])
                                pb_rel_at[par] = (sHV, sHV.n - sHV.base)
                            else:
                                E("vector",
                                  lambda e, par=par, e01=e01, tc=tc:
                                  e.tensor_tensor(bp_s[:, :], ps[1 + 2 * par][:, :],
                                                  c_rep[:, e01, tc * 512:(tc + 1) * 512],
                                                  ALU.mult),
                                  [(sPBf[par], 1)])
                                pb_rel_at[par] = (sPBf[par], sPBf[par].n - sPBf[par].base)
                                W("vector", sSAr[sslot], sa_fills[sslot])
                                E("vector",
                                  lambda e, ss=sslot, hd=hdst:
                                  e.tensor_tensor(hd, sa_s[ss][:, :], bp_s[:, :],
                                                  ALU.mult),
                                  [(sHV, 1)])
                            sa_rel_at[sslot] = sHV.n - sHV.base
                            if tc == NTC2 - 1:
                                w13_rel_at[slot] = (sHV, sHV.n - sHV.base)
                            ab_group += 1
                        jchunk += 1
                hv_target = sHV.n - sHV.base
                for dch in range(NDC4):
                    s0 = load_w2(e0, dch)
                    s1 = load_w2(e1, dch)
                    for tt in range(TT8):
                        ypar = y_group % 2
                        if py_fills[ypar] > 0:
                            W("tensor", sYEr[ypar], py_fills[ypar])
                        if tt == 0:
                            W("tensor", sW2r[s0], 16 * w2_fills[s0])
                            W("tensor", sW2r[s1], 16 * w2_fills[s1])
                            W("tensor", sHV, hv_target)
                        n_mm = 2 * NIC
                        mm_i = 0
                        for e01, ws in ((0, s0), (1, s1)):
                            for ic in range(NIC):
                                first = (mm_i == 0)
                                last = (mm_i == n_mm - 1)
                                E("tensor",
                                  lambda e, yp=ypar, e01=e01, ic=ic, tt=tt, ws=ws,
                                  f=first, l=last:
                                  e.matmul(ps[4 + yp][:, :],
                                           h_pair[:, e01, ic, tt * P:(tt + 1) * P],
                                           w2c[ws][:, ic, :], start=f, stop=l),
                                  [(sPYr[ypar], 1)] if last else ())
                                mm_i += 1
                        py_fills[ypar] += 1
                        yslot = y_group % 2
                        W("vector", sPYr[ypar], py_fills[ypar])
                        if ye_fills[yslot] > 0:
                            W("vector", sACCY, 16 * ye_rel_at[yslot])
                        E("vector",
                          lambda e, yp=ypar, ys=yslot:
                          e.tensor_copy(yev[ys][:, :], ps[4 + yp][:, :]),
                          [(sYEr[yslot], 1)])
                        ye_fills[yslot] += 1
                        W("gpsimd", sYEr[yslot], ye_fills[yslot])
                        row0 = half * HALF + tt * P
                        E("gpsimd",
                          lambda e, ys=yslot, r=row0, dch=dch:
                          e.dma_start(out=y_scr[r:r + P, dch * 512:(dch + 1) * 512],
                                      in_=yev[ys][:, :], accum_op=ALU.add),
                          [(sACCY, 16)])
                        ye_rel_at[yslot] = (sACCY.n - sACCY.base) // 16
                        if tt == TT8 - 1:
                            rel = (sYEr[ypar], sYEr[ypar].n - sYEr[ypar].base)
                            w2_rel_at[s0] = rel
                            w2_rel_at[s1] = rel
                        y_group += 1
                W("gpsimd", sACCY)
                MARK("gpsimd")
        pass
        sACCY_out = sACCY
    if _STAGE == "y":
        sFy = sem("sFy")
        W("sync", sACCY_out)
        E("sync", lambda e: e.dma_start(out=out, in_=y_scr), [(sFy, 16)])
        E("sync", lambda e: e.dma_start(out=sent, in_=sent_sb[:]), [(sFy, 16)])
        W("sync", sFy, 32)
        flush()
        st.close()
        return nc

    with contextlib.ExitStack() as stL:
        ecL = stL.enter_context
        u2 = [ecL(nc.sbuf_tensor("u2_%d" % i, [P, D], F32)) for i in (0, 1)]
        o_t = [ecL(nc.sbuf_tensor("o_t%d" % i, [P, D], F32)) for i in (0, 1)]
        sq2 = ecL(nc.sbuf_tensor("sq2", [P, D], F32))
        sum2 = ecL(nc.sbuf_tensor("sum2", [P, 1], F32))
        nm2 = ecL(nc.sbuf_tensor("nm2", [P, 1], F32))
        ssq2 = ecL(nc.sbuf_tensor("ssq2", [P, 1], F32))
        lnv2 = ecL(nc.sbuf_tensor("lnv2", [P, 1], F32))
        rstd2 = ecL(nc.sbuf_tensor("rstd2", [P, 1], F32))

        sU2 = [alias("sXT"), alias("sX1r")]
        u2_fills = [0, 0]
        sS5 = alias("sPSG")
        sOUT = alias("sWO")
        out_done = [0, 0]

        W("sync", sACCY_out)
        for tt in range(NTT):
            slot = tt % 2
            if u2_fills[slot] > 0:
                W("sync", sS5, tt - 1)
            E("sync",
              lambda e, s=slot, tt=tt:
              e.dma_start(out=u2[s][:], in_=y_scr[tt * P:(tt + 1) * P, :]),
              [(sU2[slot], 16)])
            u2_fills[slot] += 1
            W("scalar", sU2[slot], 16 * u2_fills[slot])
            if tt == 0:
                W("scalar", sC, 16 * 9)
            if out_done[slot] > 0:
                W("scalar", sOUT, 16 * out_done[slot])
            E("scalar", lambda e, s=slot:
              e.activation(sq2[:, :], u2[s][:, :], AF.Identity,
                           accum_out=sum2[:, 0:1]))
            DR("scalar")
            E("scalar", lambda e: e.mul(nm2[:, 0:1], sum2[:, 0:1], -1.0 / D))
            DR("scalar")
            E("scalar", lambda e, s=slot:
              e.activation(sq2[:, :], u2[s][:, :], AF.Square, bias=nm2[:, 0:1],
                           accum_out=ssq2[:, 0:1]))
            DR("scalar")
            E("scalar", lambda e:
              e.activation(lnv2[:, 0:1], ssq2[:, 0:1], AF.Ln, scale=1.0 / D,
                           bias=epsc_sb[:, 0:1]))
            DR("scalar")
            E("scalar", lambda e:
              e.activation(rstd2[:, 0:1], lnv2[:, 0:1], AF.Exp, scale=-0.5))
            DR("scalar")
            E("scalar", lambda e, s=slot:
              e.activation(sq2[:, :], u2[s][:, :], AF.Identity, bias=nm2[:, 0:1]))
            DR("scalar")
            E("scalar", lambda e, s=slot:
              e.activation(o_t[s][:, :], sq2[:, :], AF.Identity,
                           scale=rstd2[:, 0:1]))
            E("scalar", lambda e, s=slot:
              e.activation(o_t[s][:, :], sq2[:, :], AF.Identity,
                           scale=rstd2[:, 0:1]),
              [(sS5, 1)])
            W("sync", sS5, tt + 1)
            MARK()
            E("sync",
              lambda e, s=slot, tt=tt:
              e.dma_start(out=out[tt * P:(tt + 1) * P, :], in_=o_t[s][:, :]),
              [(sOUT, 16)])
            out_done[slot] = tt + 1
        W("sync", sOUT, 16 * NTT)
        sSE = sem("sSE") if "sSE" not in sems else sems["sSE"]
        E("sync", lambda e: e.dma_start(out=sent, in_=sent_sb[:]), [(sSE, 16)])
        W("sync", sSE, 16)
        flush()

    st.close()
    return nc



def _devfmt_w(wt):
    return np.ascontiguousarray(wt.reshape(16, P, 16, P).transpose(0, 3, 2, 1))


def kernel(L0_attn_x, L0_attn_Wq, L0_attn_bq, L0_attn_Wk, L0_attn_bk,
           L0_attn_Wv, L0_attn_bv, L0_attn_Wo, L0_attn_bo,
           L0_ffn_Wg, L0_ffn_W1, L0_ffn_W3, L0_ffn_W2,
           L0_ffn_Ws1, L0_ffn_Ws3, L0_ffn_Ws2):
    f32 = np.float32
    x = np.asarray(L0_attn_x, dtype=f32)
    Wq = np.asarray(L0_attn_Wq, f32); bq = np.asarray(L0_attn_bq, f32)
    Wk = np.asarray(L0_attn_Wk, f32); bk = np.asarray(L0_attn_bk, f32)
    Wv = np.asarray(L0_attn_Wv, f32); bv = np.asarray(L0_attn_bv, f32)
    Wo = np.asarray(L0_attn_Wo, f32); bo = np.asarray(L0_attn_bo, f32)
    Wg = np.asarray(L0_ffn_Wg, f32)
    W1 = np.asarray(L0_ffn_W1, f32); W3 = np.asarray(L0_ffn_W3, f32)
    W2 = np.asarray(L0_ffn_W2, f32)
    Ws1 = np.asarray(L0_ffn_Ws1, f32); Ws3 = np.asarray(L0_ffn_Ws3, f32)
    Ws2 = np.asarray(L0_ffn_Ws2, f32)

    wq_dev = _devfmt_w(Wq)
    wk_dev = _devfmt_w(Wk)
    wv_dev = _devfmt_w(Wv)
    wo_dev = np.ascontiguousarray(Wo.T.reshape(16, P, D).transpose(1, 0, 2))
    wg_dev = np.ascontiguousarray(Wg.T.reshape(16, P, 16).transpose(1, 0, 2))
    bq_l = np.ascontiguousarray(bq.reshape(16, P).T)
    bk_l = np.ascontiguousarray(bk.reshape(16, P).T)
    bv_l = np.ascontiguousarray(bv.reshape(16, P).T)
    bo_rep = np.ascontiguousarray(np.broadcast_to(bo, (P, D)))

    w1r = W1.reshape(16, NIC, P, 16, P).transpose(0, 1, 4, 3, 2)
    w1s = Ws1.reshape(2, NIC, P, 16, P).transpose(0, 1, 4, 3, 2)
    w1_dev = np.ascontiguousarray(np.concatenate([w1r, w1s], axis=0))
    w3r = W3.reshape(16, NIC, P, 16, P).transpose(0, 1, 4, 3, 2)
    w3s = Ws3.reshape(2, NIC, P, 16, P).transpose(0, 1, 4, 3, 2)
    w3_dev = np.ascontiguousarray(np.concatenate([w3r, w3s], axis=0))
    w2r = W2.reshape(16, NDC4, 512, NIC, P).transpose(0, 1, 4, 3, 2)
    w2s = Ws2.T.reshape(2, NIC, P, NDC4, 512).transpose(0, 3, 2, 1, 4)
    w2_dev = np.ascontiguousarray(np.concatenate([w2r, w2s], axis=0))

    ident = np.eye(P, dtype=f32)
    ones1 = np.ones((1, P), dtype=f32)
    epsc = np.full((P, 1), EPS, dtype=f32)
    sentv = np.full((P, 64), 123.0, dtype=f32)
    zpre = np.zeros((P, NDCH, NT), dtype=f32)

    shared = dict(
        wq_dev=wq_dev, wk_dev=wk_dev, wv_dev=wv_dev, wo_dev=wo_dev,
        bq_l=bq_l, bk_l=bk_l, bv_l=bv_l, bo_rep=bo_rep, wg_dev=wg_dev,
        w1_dev=w1_dev, w3_dev=w3_dev, w2_dev=w2_dev,
        ident=ident, ones1=ones1, epsc=epsc, sentv=sentv,
    )

    in_maps = []
    for c in range(NCORES):
        b, hh = divmod(c, 2)
        sl = slice(hh * NT, (hh + 1) * NT)
        xtm = np.ascontiguousarray(x[b, sl, :])
        xdev = np.ascontiguousarray(xtm.T.reshape(16, P, NT).transpose(1, 0, 2))
        if hh == 1:
            xpre = np.ascontiguousarray(
                x[b, 0:NT, :].T.reshape(16, P, NT).transpose(1, 0, 2))
            cm = np.ones((P, 1), dtype=f32)
        else:
            xpre = zpre
            cm = np.zeros((P, 1), dtype=f32)
        m = dict(shared)
        m.update(xdev=xdev, xtm=xtm, xpre=xpre, cmask=cm)
        in_maps.append(m)

    nc = build_nc()
    res = run_bass_kernel_spmd(nc, in_maps, core_ids=list(range(NCORES)))

    full = np.empty((4, 4096, D), dtype=f32)
    for c in range(NCORES):
        b, hh = divmod(c, 2)
        full[b, hh * NT:(hh + 1) * NT, :] = res.results[c]["out"]
    return full


# version 04 (final)
# speedup vs baseline: 1.0855x; 1.0524x over previous
import contextlib
import numpy as np

_PLAN = None

import concourse.bass as bass
import concourse.mybir as mybir
from concourse.bass_utils import run_bass_kernel_spmd

F32 = mybir.dt.float32
BF16 = mybir.dt.bfloat16
AF = mybir.ActivationFunctionType
ALU = mybir.AluOpType
AX = mybir.AxisListType

P = 128
D = 2048
NDCH = 16
NT = 2048
NTCH = 4
NTT = 16
INTER = 1408
NIC = 11
NEXP = 18
NPAIR = 9
NHALF = 2
HALF = 1024
TT8 = 8
NTC2 = 2
NDC4 = 4
SCALE = float(128 ** -0.5)
EPS = 1e-5
NCORES = 8

ENGINES = ("sync", "gpsimd", "vector", "scalar", "tensor")

_STAGE = "full"
_SERIAL = False
_SERIALDMA = False


class Sem:
    __slots__ = ("h", "n", "base")

    def __init__(self, h):
        self.h = h
        self.n = 0
        self.base = 0


def build_nc():
    nc = bass.Bass("TRN2", target_bir_lowering=False)

    dt = {}

    def din(name, shape, dtype=F32):
        dt[name] = nc.dram_tensor(name, list(shape), dtype, kind="ExternalInput").ap()

    din("xdev", (P, NDCH, NT))
    din("xtm", (NT, D))
    din("xpre", (P, NDCH, NT))
    din("cmask", (P, 1))
    din("wq_dev", (16, P, NDCH, P))
    din("wk_dev", (16, P, NDCH, P))
    din("wv_dev", (16, P, NDCH, P))
    din("wo_dev", (P, NDCH, D))
    din("bq_l", (P, 16))
    din("bk_l", (P, 16))
    din("bv_l", (P, 16))
    din("bo_rep", (P, D))
    din("wg_dev", (P, NDCH, 16))
    din("w1_dev", (NEXP, NIC, P, NDCH, P))
    din("w3_dev", (NEXP, NIC, P, NDCH, P))
    din("w2_dev", (NEXP, NDC4, P, NIC, 512))
    din("ident", (P, P))
    din("ones1", (1, P))
    din("epsc", (P, 1))
    din("sentv", (P, 64))

    out = nc.dram_tensor("out", [NT, D], F32, kind="ExternalOutput").ap()
    sent = nc.dram_tensor("sent", [P, 64], F32, kind="ExternalOutput").ap()
    prog = nc.dram_tensor("prog", [64, 64], F32, kind="ExternalOutput").ap()
    x1fm_scr = nc.dram_tensor("x1fm_scr", [P, NDCH, NT], BF16, kind="Internal").ap()
    y_scr = nc.dram_tensor("y_scr", [NT, D], F32, kind="Internal").ap()
    comb_hbm = nc.dram_tensor("comb_hbm", [16, NT], F32, kind="Internal").ap()

    st = contextlib.ExitStack()
    ec = st.enter_context

    ident_sb = ec(nc.sbuf_tensor("ident_sb", [P, P], F32))
    bq_sb = ec(nc.sbuf_tensor("bq_sb", [P, 16], F32))
    bk_sb = ec(nc.sbuf_tensor("bk_sb", [P, 16], F32))
    bv_sb = ec(nc.sbuf_tensor("bv_sb", [P, 16], F32))
    cmask_sb = ec(nc.sbuf_tensor("cmask_sb", [P, 1], F32))
    wg_sb = ec(nc.sbuf_tensor("wg_sb", [P, NDCH, 16], F32))
    epsc_sb = ec(nc.sbuf_tensor("epsc_sb", [P, 1], F32))
    sent_sb = ec(nc.sbuf_tensor("sent_sb", [P, 64], F32))
    ones_sb = ec(nc.sbuf_tensor("ones_sb", [1, P], F32))
    carry0 = ec(nc.sbuf_tensor("carry0", [P, 16], F32))
    carry_eff = ec(nc.sbuf_tensor("carry_eff", [P, 16], F32))
    zc = ec(nc.sbuf_tensor("zc", [P, 1], F32))
    combT = ec(nc.sbuf_tensor("combT", [16, NT], F32))
    scores_all = ec(nc.sbuf_tensor("scores_all", [P, NTT, 16], F32))
    comb_all = ec(nc.sbuf_tensor("comb_all", [P, NTT, 16], F32))
    r_gbuf = ec(nc.sbuf_tensor("r_gbuf", [P, NTT, 4], F32))
    r_m1 = ec(nc.sbuf_tensor("r_m1", [P, NTT, 1], F32))
    r_m2 = ec(nc.sbuf_tensor("r_m2", [P, NTT, 1], F32))
    r_t1 = ec(nc.sbuf_tensor("r_t1", [P, NTT, 16], F32))
    r_t2 = ec(nc.sbuf_tensor("r_t2", [P, NTT, 16], F32))
    r_pen = ec(nc.sbuf_tensor("r_pen", [P, NTT, 4], F32))
    r_sum = ec(nc.sbuf_tensor("r_sum", [P, NTT, 1], F32))

    ps = [ec(nc.psum_tensor("ps%d" % i, [P, 512], F32)) for i in range(8)]

    blk = ec(nc.Block())

    steps = {k: [] for k in ENGINES}
    sems = {}
    global _PLAN
    _PLAN = {k: [] for k in ENGINES}

    def sem(name):
        if name not in sems:
            s = Sem(ec(nc.semaphore(name)))
            _PLAN.setdefault("_names", {})[id(s.h)] = name
            sems[name] = s
        return sems[name]

    def alias(old_name):
        phys = sems[old_name]
        s = Sem(phys.h)
        s.base = phys.n
        s.n = phys.n
        return s

    def E(eng, fn, incs=()):
        for s, k in incs:
            s.n += k

        def run(e, fn=fn, incs=tuple(incs)):
            instr = fn(e)
            for s, k in incs:
                instr.then_inc(s.h, k)

        steps[eng].append(run)
        _PLAN[eng].append(("op", tuple((id(s.h), k) for s, k in incs)))

    def W(eng, s, val=None):
        v = s.n if val is None else s.base + val
        if v > 0:
            steps[eng].append(lambda e, h=s.h, v=v: e.wait_ge(h, v))
            _PLAN[eng].append(("wait", id(s.h), v))

    def DR(eng):
        steps[eng].append(lambda e: e.drain())

    def flush():
        for k in ENGINES:
            fns = steps[k]
            if not fns:
                continue
            deco = getattr(blk, k)

            def runner(e, fns=fns):
                for f in fns:
                    f(e)

            deco(runner)
            steps[k] = []

    sC = sem("sC")
    for name, tile in (
        ("ident", ident_sb), ("bq_l", bq_sb), ("bk_l", bk_sb), ("bv_l", bv_sb),
        ("cmask", cmask_sb), ("wg_dev", wg_sb), ("epsc", epsc_sb), ("ones1", ones_sb),
    ):
        E("sync", lambda e, t=tile, n=name: e.dma_start(out=t[:], in_=dt[n]), [(sC, 16)])
    E("vector", lambda e: e.memset(zc[:], 0.0))
    E("sync", lambda e: e.dma_start(out=sent_sb[:], in_=dt["sentv"]), [(sC, 16)])

    mk_n = [0]
    sMRK = None

    def MARK(eng="sync"):
        nonlocal sMRK
        if sMRK is None:
            sMRK = sem("sMRK")
        k = mk_n[0]
        mk_n[0] += 1
        W(eng, sem("sC"), 16 * 9)
        E(eng, lambda e, k=k: e.dma_start(out=prog[k:k + 1, :], in_=sent_sb[0:1, :]),
          [(sMRK, 16)])

    sVATT = sem("sVATT")
    sP3V = sem("sP3V")

    stCtx = contextlib.ExitStack()
    ctx_b = stCtx.enter_context(nc.sbuf_tensor("ctx_b", [P, NDCH, NT], BF16))

    stXA = contextlib.ExitStack()
    x_arena = stXA.enter_context(nc.sbuf_tensor("x_arena", [P, NDCH, NT], BF16))
    with contextlib.ExitStack() as stA:
        ecA = stA.enter_context
        wslot = {(pr, s): ecA(nc.sbuf_tensor("w%s%d" % (pr, s), [P, NDCH, P], BF16))
                 for pr in ("q", "k", "v") for s in (0, 1)}
        pkv_e = ecA(nc.sbuf_tensor("pkv_e", [P, NT], F32))
        phiq_e = ecA(nc.sbuf_tensor("phiq_e", [P, NT], F32))
        mt_s = {(pr, s): ecA(nc.sbuf_tensor("mt%s%d" % (pr, s), [P, 512], F32))
                for pr in ("q", "k") for s in (0, 1)}
        ex_s = {(pr, s): ecA(nc.sbuf_tensor("ex%s%d" % (pr, s), [P, 512], F32))
                for pr in ("q", "k") for s in (0, 1)}
        vp_s = ecA(nc.sbuf_tensor("vp_s", [P, 512], F32))

        sXA = sem("sXA")
        sPFX = sem("sPFX")
        sVT = sem("sVT")
        sWR = {k: sem("sWR_%s%d" % k) for k in wslot}
        wfills = {k: 0 for k in wslot}
        w_rel_at = {k: None for k in wslot}
        bank = {("q", 0): 0, ("q", 1): 3, ("k", 0): 1, ("k", 1): 4,
                ("v", 0): 2, ("v", 1): 5}
        sBR = {k: sem("sBR_%s%d" % k) for k in bank}
        sBF = {k: sem("sBF_%s%d" % k) for k in bank}
        bfills = {k: 0 for k in bank}
        sMTr = {k: sem("sMTr_%s%d" % k) for k in mt_s}
        mtfills = {k: 0 for k in mt_s}
        sEXr = {k: sem("sEXr_%s%d" % k) for k in ex_s}
        sEXf = {k: sem("sEXf_%s%d" % k) for k in ex_s}
        exfills = {k: 0 for k in ex_s}

        def load_w(pr, ech, src):
            s = ech % 2
            k = (pr, s)
            if _SERIALDMA:
                W("gpsimd", sVT)
            if w_rel_at[k] is not None:
                W("gpsimd", w_rel_at[k][0], w_rel_at[k][1])
            E("gpsimd",
              lambda e, t=wslot[k], n=src, ech=ech: e.dma_start(out=t[:], in_=dt[n][ech]),
              [(sWR[k], 16)])
            wfills[k] += 1

        def proj_mms(pr, ech, g, tch, release_w):
            s = ech % 2
            par = g % 2
            bnk = bank[(pr, par)]
            W("tensor", sWR[(pr, s)], 16 * wfills[(pr, s)])
            if bfills[(pr, par)] > 0:
                W("tensor", sBF[(pr, par)], bfills[(pr, par)])
            for dch in range(NDCH):
                E("tensor",
                  lambda e, bnk=bnk, w=wslot[(pr, s)], dch=dch, tch=tch:
                  e.matmul(ps[bnk][:, :], w[:, dch, :],
                           x_arena[:, dch, tch * 512:(tch + 1) * 512],
                           start=(dch == 0), stop=(dch == NDCH - 1)),
                  [(sBR[(pr, par)], 1)] if dch == NDCH - 1 else ())
            bfills[(pr, par)] += 1
            if release_w:
                w_rel_at[(pr, s)] = (sBF[(pr, par)], bfills[(pr, par)])

        def phi_chain(pr, ech, g, bias_sb, dest):
            par = g % 2
            bnk = bank[(pr, par)]
            mk = (pr, par)
            W("vector", sBR[(pr, par)], bfills[(pr, par)])
            if mtfills[mk] > 0:
                W("vector", sEXr[mk], mtfills[mk])
            E("vector",
              lambda e, bnk=bnk, mk=mk, b=bias_sb, ech=ech:
              e.tensor_scalar(mt_s[mk][:, :], ps[bnk][:, :], b[:, ech:ech + 1], 0.0,
                              ALU.add, ALU.min),
              [(sMTr[mk], 1)])
            mtfills[mk] += 1
            W("scalar", sMTr[mk], mtfills[mk])
            if exfills[mk] > 0:
                W("scalar", sEXf[mk], exfills[mk])
            E("scalar",
              lambda e, mk=mk: e.activation(ex_s[mk][:, :], mt_s[mk][:, :], AF.Exp),
              [(sEXr[mk], 1)])
            exfills[mk] += 1
            E("vector",
              lambda e, bnk=bnk, b=bias_sb, ech=ech, d=dest:
              e.tensor_scalar(d, ps[bnk][:, :], b[:, ech:ech + 1], 0.0, ALU.add, ALU.max),
              [(sBF[(pr, par)], 1)])
            W("vector", sEXr[mk], exfills[mk])
            E("vector",
              lambda e, mk=mk, d=dest: e.tensor_tensor(d, d, ex_s[mk][:, :], ALU.add),
              [(sEXf[mk], 1)])

        def v_evac(ech, g):
            par = g % 2
            bnk = bank[("v", par)]
            W("vector", sBR[("v", par)], bfills[("v", par)])
            E("vector",
              lambda e, bnk=bnk, ech=ech:
              e.tensor_scalar(vp_s[:, :], ps[bnk][:, :], bv_sb[:, ech:ech + 1],
                              None, ALU.add),
              [(sBF[("v", par)], 1)])

        E("gpsimd", lambda e: e.dma_start(out=x_arena[:], in_=dt["xpre"]), [(sXA, 16)])
        load_w("k", 0, "wk_dev")
        load_w("v", 0, "wv_dev")
        g = 0
        for ech in range(16):
            if ech + 1 < 16:
                load_w("k", ech + 1, "wk_dev")
                load_w("v", ech + 1, "wv_dev")
            if ech == 0:
                W("tensor", sXA, 16)
                W("vector", sC, 16 * 9)
            for tch in range(NTCH):
                rel = (tch == NTCH - 1)
                if _SERIAL:
                    W("tensor", sVT)
                proj_mms("k", ech, g, tch, rel)
                proj_mms("v", ech, g, tch, rel)
                dest = pkv_e[:, tch * 512:(tch + 1) * 512]
                phi_chain("k", ech, g, bk_sb, dest)
                v_evac(ech, g)
                E("vector", lambda e, d=dest: e.tensor_tensor(d, d, vp_s[:, :], ALU.mult),
                  [(sVT, 1)])
                g += 1
            E("vector",
              lambda e, ech=ech:
              e.reduce_sum(carry0[:, ech:ech + 1], pkv_e[:, :], axis=AX.X))
        DR("vector")
        E("vector",
          lambda e: e.tensor_scalar(carry_eff[:, :], carry0[:, :], cmask_sb[:, 0:1],
                                    None, ALU.mult),
          [(sPFX, 1)])
        DR("vector")

        W("gpsimd", sPFX, 1)
        E("gpsimd", lambda e: e.dma_start(out=x_arena[:], in_=dt["xdev"]), [(sXA, 16)])
        load_w("q", 0, "wq_dev")
        load_w("k", 0, "wk_dev")
        load_w("v", 0, "wv_dev")
        for ech in range(16):
            if ech + 1 < 16:
                load_w("q", ech + 1, "wq_dev")
                load_w("k", ech + 1, "wk_dev")
                load_w("v", ech + 1, "wv_dev")
            if ech == 0:
                W("tensor", sXA, 32)
            for tch in range(NTCH):
                rel = (tch == NTCH - 1)
                if _SERIAL:
                    W("tensor", sVT)
                proj_mms("q", ech, g, tch, rel)
                proj_mms("k", ech, g, tch, rel)
                proj_mms("v", ech, g, tch, rel)
                phi_chain("q", ech, g, bq_sb, phiq_e[:, tch * 512:(tch + 1) * 512])
                dest = pkv_e[:, tch * 512:(tch + 1) * 512]
                phi_chain("k", ech, g, bk_sb, dest)
                v_evac(ech, g)
                E("vector", lambda e, d=dest: e.tensor_tensor(d, d, vp_s[:, :], ALU.mult),
                  [(sVT, 1)])
                g += 1
            E("vector",
              lambda e, ech=ech:
              e.tensor_tensor_scan(pkv_e[:, :], pkv_e[:, :],
                                   zc[:, 0:1].broadcast_to([P, NT]),
                                   carry_eff[:, ech:ech + 1], ALU.add, ALU.add))
            DR("vector")
            E("vector",
              lambda e, ech=ech:
              e.scalar_tensor_tensor(ctx_b[:, ech, :], phiq_e[:, :], SCALE, pkv_e[:, :],
                                     ALU.mult, ALU.mult),
              [(sVATT, 1)])

        if _STAGE == "ctx":
            sF = sem("sF")
            W("gpsimd", sVATT, 16)
            E("gpsimd",
              lambda e: e.dma_start(out=out.rearrange("(c p) t -> p c t", p=P),
                                    in_=ctx_b[:]), [(sF, 16)])
            W("sync", sF, 16)
            E("sync", lambda e: e.dma_start(out=sent, in_=sent_sb[:]), [(sem("sFs"), 16)])
            W("sync", sem("sFs"), 16)
        bank_sems_final = [(sBR[k], sBR[k].n) for k in bank]
        pass
    stXA.close()
    if _STAGE == "ctx":
        flush()
        stCtx.close()
        st.close()
        return nc

    with contextlib.ExitStack() as stB:
        ecB = stB.enter_context
        woT_b = ecB(nc.sbuf_tensor("woT_b", [P, NDCH, D], BF16))
        borep_sb = ecB(nc.sbuf_tensor("borep_sb", [P, D], F32))
        u_t = ecB(nc.sbuf_tensor("u_t", [P, D], F32))
        xbo = ecB(nc.sbuf_tensor("xbo", [P, D], F32))
        xtm_t = ecB(nc.sbuf_tensor("xtm_t", [P, D], F32))
        x1tm_t = ecB(nc.sbuf_tensor("x1tm_t", [P, D], F32))
        x1f32 = ecB(nc.sbuf_tensor("x1f32", [P, NDCH, P], F32))
        x1b16 = ecB(nc.sbuf_tensor("x1b16", [P, NDCH, P], BF16))
        msum = ecB(nc.sbuf_tensor("msum", [P, 4], F32))
        mred = ecB(nc.sbuf_tensor("mred", [P, 1], F32))
        meanb = ecB(nc.sbuf_tensor("meanb", [P, 1], F32))
        m2b = ecB(nc.sbuf_tensor("m2b", [P, 1], F32))
        ssqb = ecB(nc.sbuf_tensor("ssqb", [P, 1], F32))
        varb = ecB(nc.sbuf_tensor("varb", [P, 1], F32))
        lnvb = ecB(nc.sbuf_tensor("lnvb", [P, 1], F32))
        rstdb = ecB(nc.sbuf_tensor("rstdb", [P, 1], F32))

        sWO = sem("sWO")
        sBOR = sem("sBOR")
        sXT = sem("sXT")
        sXTf = sem("sXTf")
        xt_fills = 0
        sX1r = sem("sX1r")
        sX1f = sem("sX1f")
        x1_fills = 0
        sPSA = [sem("sPSA%d" % i) for i in range(4)]
        sPSAf = [sem("sPSAf%d" % i) for i in range(4)]
        psa_fills = [0, 0, 0, 0]
        psa_fill_at = {}
        sTRr = [sem("sTRr%d" % i) for i in (0, 1)]
        sTRE = sem("sTRE")
        tr_fills = [0, 0]
        tr_rel_at = {0: 0, 1: 0}
        sPEG = sem("sPEG")
        sPSG = sem("sPSG")
        sPSGf = sem("sPSGf")
        sVLN = sem("sVLN")
        sRSTD = sem("sRSTD")
        sYW = sem("sYW")
        sXW = sem("sXW")
        sX1B = sem("sX1B")

        W("gpsimd", sVATT, 16)
        for cpart in range(4):
            E("gpsimd",
              lambda e, c=cpart:
              e.dma_start(out=woT_b[:, c * 4:(c + 1) * 4, :],
                          in_=dt["wo_dev"][:, c * 4:(c + 1) * 4, :]),
              [(sWO, 16)])
        W("sync", sVATT, 16)
        MARK()
        E("sync", lambda e: e.dma_start(out=borep_sb[:], in_=dt["bo_rep"]), [(sBOR, 16)])
        E("sync", lambda e: e.dma_start(out=xtm_t[:], in_=dt["xtm"][0:P, :]), [(sXT, 16)])
        xt_fills += 1

        def emit_wo(tt):
            if tt == 0:
                W("tensor", sWO, 64)
                W("tensor", sVATT, 16)
            for dc in range(NDC4):
                par = (tt * NDC4 + dc) % 4
                if psa_fills[par] > 0:
                    W("tensor", sPSAf[par], psa_fills[par])
                for dch in range(NDCH):
                    E("tensor",
                      lambda e, par=par, dch=dch, tt=tt, dc=dc:
                      e.matmul(ps[par][:, :], ctx_b[:, dch, tt * P:(tt + 1) * P],
                               woT_b[:, dch, dc * 512:(dc + 1) * 512],
                               start=(dch == 0), stop=(dch == NDCH - 1)),
                      [(sPSA[par], 1)] if dch == NDCH - 1 else ())
                psa_fills[par] += 1
                psa_fill_at[(tt, dc)] = psa_fills[par]

        def emit_ln1(tt):
            nonlocal xt_fills, x1_fills
            W("vector", sXT, 16 * xt_fills)
            if tt == 0:
                W("vector", sBOR, 16)
            E("vector",
              lambda e: e.tensor_tensor(xbo[:, :], xtm_t[:, :], borep_sb[:, :], ALU.add),
              [(sXTf, 1)])
            if tt + 1 < NTT:
                W("sync", sXTf, xt_fills)
                E("sync",
                  lambda e, tt=tt:
                  e.dma_start(out=xtm_t[:], in_=dt["xtm"][(tt + 1) * P:(tt + 2) * P, :]),
                  [(sXT, 16)])
                xt_fills += 1
            for dc in range(NDC4):
                par = (tt * NDC4 + dc) % 4
                W("vector", sPSA[par], psa_fill_at[(tt, dc)])
                E("vector",
                  lambda e, par=par, dc=dc:
                  e.scalar_tensor_tensor(u_t[:, dc * 512:(dc + 1) * 512], ps[par][:, :],
                                         0.0, xbo[:, dc * 512:(dc + 1) * 512],
                                         ALU.add, ALU.add, accum_out=msum[:, dc:dc + 1]),
                  [(sPSAf[par], 1)])
            DR("vector")
            E("vector", lambda e: e.reduce_sum(mred[:, 0:1], msum[:, :], axis=AX.X))
            DR("vector")
            E("vector", lambda e: e.tensor_scalar(meanb[:, 0:1], mred[:, 0:1],
                                                  1.0 / D, None, ALU.mult))
            E("vector",
              lambda e: e.scalar_tensor_tensor(xbo[:, :], u_t[:, :], 1.0, u_t[:, :],
                                               ALU.mult, ALU.mult,
                                               accum_out=ssqb[:, 0:1]))
            DR("vector")
            E("vector", lambda e: e.tensor_tensor(m2b[:, 0:1], meanb[:, 0:1],
                                                  meanb[:, 0:1], ALU.mult))
            E("vector",
              lambda e: e.scalar_tensor_tensor(varb[:, 0:1], ssqb[:, 0:1], 1.0 / D,
                                               m2b[:, 0:1], ALU.mult, ALU.subtract))
            E("vector",
              lambda e: e.scalar_tensor_tensor(varb[:, 0:1], ssqb[:, 0:1], 1.0 / D,
                                               m2b[:, 0:1], ALU.mult, ALU.subtract),
              [(sVLN, 1)])
            W("scalar", sVLN)
            E("scalar", lambda e: e.activation(lnvb[:, 0:1], varb[:, 0:1], AF.Ln,
                                               bias=epsc_sb[:, 0:1]))
            DR("scalar")
            E("scalar", lambda e: e.activation(rstdb[:, 0:1], lnvb[:, 0:1], AF.Exp,
                                               scale=-0.5))
            E("scalar", lambda e: e.activation(rstdb[:, 0:1], lnvb[:, 0:1], AF.Exp,
                                               scale=-0.5), [(sRSTD, 1)])
            if x1_fills > 0:
                W("vector", sYW, 16 * x1_fills)
            W("vector", sRSTD)
            E("vector",
              lambda e:
              e.tensor_scalar(x1tm_t[:, :], u_t[:, :], meanb[:, 0:1], rstdb[:, 0:1],
                              ALU.subtract, ALU.mult),
              [(sX1r, 1)])
            x1_fills += 1
            W("sync", sX1r, x1_fills)
            MARK()
            E("sync",
              lambda e, tt=tt:
              e.dma_start(out=y_scr[tt * P:(tt + 1) * P, :], in_=x1tm_t[:, :]),
              [(sYW, 16)])

        def emit_tr_gate(tt):
            W("tensor", sX1r, x1_fills)
            if tt > 0:
                W("tensor", sPSGf, tt)
            for grp in range(4):
                tp = grp % 2
                if tr_fills[tp] > 0:
                    W("tensor", sTRE, tr_rel_at[tp])
                for j in range(4):
                    dch = grp * 4 + j
                    E("tensor",
                      lambda e, tp=tp, j=j, dch=dch:
                      e.transpose(ps[4 + tp][:, j * P:(j + 1) * P],
                                  x1tm_t[:, dch * P:(dch + 1) * P], ident_sb[:]),
                      [(sTRr[tp], 1)] if j == 3 else ())
                tr_fills[tp] += 1
                W("vector", sTRr[tp], tr_fills[tp])
                E("vector",
                  lambda e, tp=tp, grp=grp:
                  e.tensor_copy(x1f32[:, grp * 4:(grp + 1) * 4, :], ps[4 + tp][:, :]),
                  [(sTRE, 1)])
                tr_rel_at[tp] = sTRE.n
            if tt > 0:
                W("vector", sXW, 16 * tt)
            E("vector", lambda e: e.tensor_copy(x1b16[:, :, :], x1f32[:, :, :]),
              [(sX1B, 1)])
            W("sync", sX1B)
            E("sync",
              lambda e, tt=tt:
              e.dma_start(out=x1fm_scr[:, :, tt * P:(tt + 1) * P], in_=x1b16[:, :, :]),
              [(sXW, 16)])
            W("tensor", sTRE)
            for dch in range(NDCH):
                E("tensor",
                  lambda e, dch=dch:
                  e.matmul(ps[7][:, 0:16], x1f32[:, dch, :], wg_sb[:, dch, :],
                           start=(dch == 0), stop=(dch == NDCH - 1)),
                  [(sPSG, 1)] if dch == NDCH - 1 else ())
            W("vector", sPSG)
            E("vector",
              lambda e, tt=tt: e.tensor_copy(scores_all[:, tt, :], ps[7][:, 0:16]),
              [(sPSGf, 1)])

        emit_wo(0)
        for tt in range(NTT):
            if tt + 1 < NTT:
                emit_wo(tt + 1)
            emit_ln1(tt)
            emit_tr_gate(tt)

        sEXPR = sem("sEXPR")
        sEXPRf = sem("sEXPRf")
        sROUT = sem("sROUT")
        s4 = scores_all[:, :, :].rearrange("p t (g f) -> p t g f", f=4)
        t14 = r_t1[:, :, :].rearrange("p t (g f) -> p t g f", f=4)
        t24 = r_t2[:, :, :].rearrange("p t (g f) -> p t g f", f=4)

        def RV(fn, incs=()):
            E("vector", fn, incs)

        RV(lambda e: e.reduce_max(r_m1[:, :, 0:1], scores_all[:, :, :], axis=AX.X))
        DR("vector")
        RV(lambda e: e.tensor_scalar(r_m1[:, :, 0:1], r_m1[:, :, 0:1], -1.0, None, ALU.mult))
        RV(lambda e: e.tensor_tensor(r_t1[:, :, :], scores_all[:, :, :],
                                     r_m1[:, :, 0:1].broadcast_to([P, NTT, 16]), ALU.add),
           [(sEXPR, 1)])
        W("scalar", sEXPR)
        E("scalar",
          lambda e: e.activation(r_t2[:, :, :].rearrange("p t e -> p (t e)"),
                                 r_t1[:, :, :].rearrange("p t e -> p (t e)"), AF.Exp),
          [(sEXPRf, 1)])
        W("vector", sEXPRf)
        RV(lambda e: e.reduce_sum(r_sum[:, :, 0:1], r_t2[:, :, :], axis=AX.X))
        DR("vector")
        RV(lambda e: e.reciprocal(r_sum[:, :, 0:1].rearrange("p t o -> p (t o)"),
                                  r_sum[:, :, 0:1].rearrange("p t o -> p (t o)")))
        DR("vector")
        RV(lambda e: e.tensor_tensor(scores_all[:, :, :], r_t2[:, :, :],
                                     r_sum[:, :, 0:1].broadcast_to([P, NTT, 16]), ALU.mult))
        RV(lambda e: e.tensor_tensor(r_gbuf[:, :, :], s4[:, :, :, 0], s4[:, :, :, 1], ALU.max))
        RV(lambda e: e.tensor_tensor(r_pen[:, :, :], s4[:, :, :, 2], s4[:, :, :, 3], ALU.max))
        RV(lambda e: e.tensor_tensor(r_gbuf[:, :, :], r_gbuf[:, :, :], r_pen[:, :, :], ALU.max))
        RV(lambda e: e.reduce_max(r_m1[:, :, 0:1], r_gbuf[:, :, :], axis=AX.X))
        DR("vector")
        RV(lambda e: e.tensor_tensor(r_pen[:, :, :], r_gbuf[:, :, :],
                                     r_m1[:, :, 0:1].broadcast_to([P, NTT, 4]), ALU.is_ge))
        RV(lambda e: e.scalar_tensor_tensor(r_pen[:, :, :], r_pen[:, :, :], -1e30,
                                            r_gbuf[:, :, :], ALU.mult, ALU.add))
        RV(lambda e: e.reduce_max(r_m2[:, :, 0:1], r_pen[:, :, :], axis=AX.X))
        DR("vector")
        RV(lambda e: e.tensor_tensor(r_gbuf[:, :, :], r_gbuf[:, :, :],
                                     r_m2[:, :, 0:1].broadcast_to([P, NTT, 4]), ALU.is_ge))
        RV(lambda e: e.tensor_scalar(r_pen[:, :, :], r_gbuf[:, :, :], 1.0, 1e30,
                                     ALU.subtract, ALU.mult))
        for f in range(4):
            RV(lambda e, f=f: e.tensor_tensor(t14[:, :, :, f], s4[:, :, :, f],
                                              r_pen[:, :, :], ALU.add))
        for _ in range(3):
            RV(lambda e: e.reduce_max(r_m1[:, :, 0:1], r_t1[:, :, :], axis=AX.X))
            DR("vector")
            RV(lambda e: e.tensor_tensor(r_t2[:, :, :], r_t1[:, :, :],
                                         r_m1[:, :, 0:1].broadcast_to([P, NTT, 16]),
                                         ALU.is_ge))
            RV(lambda e: e.scalar_tensor_tensor(r_t1[:, :, :], r_t2[:, :, :], -1e30,
                                                r_t1[:, :, :], ALU.mult, ALU.add))
        RV(lambda e: e.reduce_max(r_m2[:, :, 0:1], r_t1[:, :, :], axis=AX.X))
        DR("vector")
        for f in range(4):
            RV(lambda e, f=f: e.tensor_tensor(t24[:, :, :, f], s4[:, :, :, f],
                                              r_pen[:, :, :], ALU.add))
        RV(lambda e: e.tensor_tensor(r_t2[:, :, :], r_t2[:, :, :],
                                     r_m2[:, :, 0:1].broadcast_to([P, NTT, 16]), ALU.is_ge))
        RV(lambda e: e.tensor_tensor(comb_all[:, :, :], scores_all[:, :, :],
                                     r_t2[:, :, :], ALU.mult),
           [(sROUT, 1)])

        sCTr = [sem("sCTr%d" % i) for i in (0, 1)]
        ct_fills = [0, 0]
        ct_rel_at = {0: 0, 1: 0}
        sCBW = sem("sCBW")
        sCB = sem("sCB")
        W("tensor", sROUT)
        for tt in range(NTT):
            par = tt % 2
            if ct_fills[par] > 0:
                W("tensor", sCBW, ct_rel_at[par])
            E("tensor",
              lambda e, par=par, tt=tt:
              e.transpose(ps[4 + par][0:16, 0:P], comb_all[:, tt, :], ident_sb[:]),
              [(sCTr[par], 1)])
            ct_fills[par] += 1
            W("vector", sCTr[par], ct_fills[par])
            E("vector",
              lambda e, par=par, tt=tt:
              e.tensor_copy(combT[0:16, tt * P:(tt + 1) * P], ps[4 + par][0:16, 0:P]),
              [(sCBW, 1)])
            ct_rel_at[par] = sCBW.n
        W("sync", sCBW, NTT)
        MARK()
        E("sync", lambda e: e.dma_start(out=comb_hbm, in_=combT[:, :]), [(sCB, 16)])
        E("vector", lambda e: e.engine_nop(), [(sP3V, 1)])

        if _STAGE in ("x1", "comb"):
            sF = sem("sF")
            if _STAGE == "x1":
                W("sync", sYW, 16 * NTT)
                E("sync", lambda e: e.dma_start(out=out, in_=y_scr), [(sF, 16)])
                E("sync", lambda e: e.dma_start(out=sent, in_=sent_sb[:]), [(sF, 16)])
            else:
                W("sync", sem("sCB"), 16)
                E("sync",
                  lambda e: e.dma_start(
                      out=out[0:P, 0:NTT * 16].rearrange("p (t e) -> p t e", e=16),
                      in_=comb_all[:, :, :]), [(sF, 16)])
            W("sync", sF, 32 if _STAGE == "x1" else 16)
        pass
    stCtx.close()
    if _STAGE in ("x1", "comb"):
        flush()
        st.close()
        return nc

    sACCY = None
    with contextlib.ExitStack() as stM:
        ecM = stM.enter_context
        x1h_b = ecM(nc.sbuf_tensor("x1h_b", [P, NDCH, HALF], BF16))
        h_pair = ecM(nc.sbuf_tensor("h_pair", [P, 2, NIC, HALF], BF16))
        w13 = {(w, s): ecM(nc.sbuf_tensor("w%d_%d" % (w, s), [P, NDCH, P], BF16))
               for w in (1, 3) for s in (0, 1)}
        w2c = [ecM(nc.sbuf_tensor("w2c%d" % s, [P, NIC, 512], BF16)) for s in range(3)]
        c_rep = ecM(nc.sbuf_tensor("c_rep", [P, 2, HALF], F32))
        sa_s = [ecM(nc.sbuf_tensor("sa_s%d" % i, [P, 512], F32)) for i in (0, 1)]
        cr_s = [ecM(nc.sbuf_tensor("cr_s%d" % i, [1, 2 * NT], F32)) for i in (0, 1)]
        bp_s = ecM(nc.sbuf_tensor("bp_s", [P, 512], F32))
        yev = [ecM(nc.sbuf_tensor("yev%d" % i, [P, 512], F32)) for i in (0, 1)]

        sX1H = alias("sXA")
        sW13r = {(1, 0): alias("sWR_q0"), (1, 1): alias("sWR_q1"),
                 (3, 0): alias("sWR_k0"), (3, 1): alias("sWR_k1")}
        w13_fills = {k: 0 for k in w13}
        w13_rel_at = [None, None]
        sW2r = [alias("sWR_v0"), alias("sWR_v1"), alias("sBR_q0")]
        w2_fills = [0, 0, 0]
        w2_rel_at = [None, None, None]
        sPAr = [alias("sBR_k0"), alias("sBR_k1")]
        pa_fills = [0, 0]
        sPBr = [alias("sBR_v0"), alias("sBR_v1")]
        sPBf = [alias("sBF_k0"), alias("sBF_k1")]
        pb_fills = [0, 0]
        pb_rel_at = [None, None]
        sPYr = [alias("sBF_v0"), alias("sBF_v1")]
        py_fills = [0, 0]
        sPCr = alias("sEXr_q0")
        sPCf = alias("sEXr_q1")
        pc_fills = 0
        sCRr = [alias("sEXr_k0"), alias("sEXr_k1")]
        cr_fills = [0, 0]
        cr_rel_at = [None, None]
        sSAr = [alias("sMTr_q0"), alias("sMTr_q1")]
        sa_fills = [0, 0]
        sa_rel_at = [0, 0]
        sYEr = [alias("sMTr_k0"), alias("sMTr_k1")]
        ye_fills = [0, 0]
        ye_rel_at = [0, 0]
        sHV = alias("sEXf_k0")
        sACCY = alias("sEXf_k1")

        chunk_seq = []
        for half in range(NHALF):
            for pair in range(NPAIR):
                for e01 in (0, 1):
                    eidx = 2 * pair + e01
                    for ic in range(NIC):
                        chunk_seq.append((eidx, ic))
        n_chunks = len(chunk_seq)

        def load_w13(j, first=False):
            eidx, ic = chunk_seq[j]
            slot = j % 2
            if first:
                W("gpsimd", sP3V, 1)
                W("gpsimd", sXW, 16 * NTT)
                W("gpsimd", sYW, 16 * NTT)
            elif w13_rel_at[slot] is not None:
                W("gpsimd", w13_rel_at[slot][0], w13_rel_at[slot][1])
            for w in (1, 3):
                k = (w, slot)
                E("gpsimd",
                  lambda e, k=k, w=w, eidx=eidx, ic=ic:
                  e.dma_start(out=w13[k][:], in_=dt["w%d_dev" % w][eidx, ic]),
                  [(sW13r[k], 16)])
                w13_fills[k] += 1

        w2_next = 0

        def load_w2(eidx, dch):
            nonlocal w2_next
            s = w2_next % 3
            w2_next += 1
            if w2_rel_at[s] is not None:
                W("gpsimd", w2_rel_at[s][0], w2_rel_at[s][1])
            E("gpsimd",
              lambda e, s=s, eidx=eidx, dch=dch:
              e.dma_start(out=w2c[s][:], in_=dt["w2_dev"][eidx, dch]),
              [(sW2r[s], 16)])
            w2_fills[s] += 1
            return s

        load_w13(0, first=True)
        ab_group = 0
        y_group = 0
        jchunk = 0

        for half in range(NHALF):
            if half == 0:
                W("sync", sXW, 16 * NTT)
                W("sync", sYW, 16 * NTT)
                W("sync", sP3V, 1)
            else:
                W("sync", sHV, sHV.n - sHV.base)
            E("sync",
              lambda e, half=half:
              e.dma_start(out=x1h_b[:],
                          in_=x1fm_scr[:, :, half * HALF:(half + 1) * HALF]),
              [(sX1H, 16)])
            if half == 0:
                W("gpsimd", sYW, 16 * NTT)

            for pair in range(NPAIR):
                shared = (pair == NPAIR - 1)
                e0, e1 = 2 * pair, 2 * pair + 1
                if not shared:
                    crslot = (half * NPAIR + pair) % 2
                    if cr_rel_at[crslot] is not None:
                        W("sync", cr_rel_at[crslot][0], cr_rel_at[crslot][1])
                    if half == 0 and pair == 0:
                        W("sync", sCB, 16)
                    for e01, eidx in ((0, e0), (1, e1)):
                        E("sync",
                          lambda e, s=crslot, e01=e01, eidx=eidx:
                          e.dma_start(out=cr_s[s][0:1, e01 * NT:(e01 + 1) * NT],
                                      in_=comb_hbm[eidx:eidx + 1, :]),
                          [(sCRr[crslot], 16)])
                    cr_fills[crslot] += 1
                    W("tensor", sCRr[crslot], 32 * cr_fills[crslot])
                    nmm = 0
                    for e01 in (0, 1):
                        for tc in range(NTC2):
                            if pc_fills > 0:
                                W("tensor", sPCf, pc_fills)
                            off = e01 * NT + half * HALF + tc * 512
                            nmm += 1
                            E("tensor",
                              lambda e, s=crslot, off=off:
                              e.matmul(ps[6][:, :], ones_sb[0:1, :],
                                       cr_s[s][0:1, off:off + 512],
                                       start=True, stop=True),
                              [(sPCr, 1)])
                            pc_fills += 1
                            W("vector", sPCr, pc_fills)
                            E("vector",
                              lambda e, e01=e01, tc=tc:
                              e.tensor_copy(c_rep[:, e01, tc * 512:(tc + 1) * 512],
                                            ps[6][:, :]),
                              [(sPCf, 1)])
                            if nmm == 4:
                                cr_rel_at[crslot] = (sPCf, sPCf.n - sPCf.base)
                for e01 in (0, 1):
                    for ic in range(NIC):
                        slot = jchunk % 2
                        if jchunk + 1 < n_chunks:
                            load_w13(jchunk + 1)
                        k1 = (1, slot)
                        k3 = (3, slot)
                        for tc in range(NTC2):
                            par = ab_group % 2
                            W("tensor", sW13r[k1], 16 * w13_fills[k1])
                            if ab_group == 0:
                                W("tensor", sX1H, 16)
                            if half == 1 and pair == 0 and e01 == 0 and ic == 0 and tc == 0:
                                W("tensor", sX1H, 32)
                            if pa_fills[par] > 0:
                                W("tensor", sSAr[par], pa_fills[par])
                            for dch in range(NDCH):
                                E("tensor",
                                  lambda e, par=par, k1=k1, dch=dch, tc=tc:
                                  e.matmul(ps[0 + 2 * par][:, :], w13[k1][:, dch, :],
                                           x1h_b[:, dch, tc * 512:(tc + 1) * 512],
                                           start=(dch == 0), stop=(dch == NDCH - 1)),
                                  [(sPAr[par], 1)] if dch == NDCH - 1 else ())
                            pa_fills[par] += 1
                            W("tensor", sW13r[k3], 16 * w13_fills[k3])
                            if pb_fills[par] > 0 and pb_rel_at[par] is not None:
                                W("tensor", pb_rel_at[par][0], pb_rel_at[par][1])
                            for dch in range(NDCH):
                                E("tensor",
                                  lambda e, par=par, k3=k3, dch=dch, tc=tc:
                                  e.matmul(ps[1 + 2 * par][:, :], w13[k3][:, dch, :],
                                           x1h_b[:, dch, tc * 512:(tc + 1) * 512],
                                           start=(dch == 0), stop=(dch == NDCH - 1)),
                                  [(sPBr[par], 1)] if dch == NDCH - 1 else ())
                            pb_fills[par] += 1
                            sslot = ab_group % 2
                            W("scalar", sPAr[par], pa_fills[par])
                            if sa_fills[sslot] > 0:
                                W("scalar", sHV, sa_rel_at[sslot])
                            E("scalar",
                              lambda e, par=par, ss=sslot:
                              e.activation(sa_s[ss][:, :], ps[0 + 2 * par][:, :], AF.Silu),
                              [(sSAr[sslot], 1)])
                            sa_fills[sslot] += 1
                            hdst = h_pair[:, e01, ic, tc * 512:(tc + 1) * 512]
                            W("vector", sPBr[par], pb_fills[par])
                            if shared:
                                W("vector", sSAr[sslot], sa_fills[sslot])
                                E("vector",
                                  lambda e, par=par, ss=sslot, hd=hdst:
                                  e.tensor_tensor(hd, sa_s[ss][:, :],
                                                  ps[1 + 2 * par][:, :], ALU.mult),
                                  [(sHV, 1)])
                                pb_rel_at[par] = (sHV, sHV.n - sHV.base)
                            else:
                                E("vector",
                                  lambda e, par=par, e01=e01, tc=tc:
                                  e.tensor_tensor(bp_s[:, :], ps[1 + 2 * par][:, :],
                                                  c_rep[:, e01, tc * 512:(tc + 1) * 512],
                                                  ALU.mult),
                                  [(sPBf[par], 1)])
                                pb_rel_at[par] = (sPBf[par], sPBf[par].n - sPBf[par].base)
                                W("vector", sSAr[sslot], sa_fills[sslot])
                                E("vector",
                                  lambda e, ss=sslot, hd=hdst:
                                  e.tensor_tensor(hd, sa_s[ss][:, :], bp_s[:, :],
                                                  ALU.mult),
                                  [(sHV, 1)])
                            sa_rel_at[sslot] = sHV.n - sHV.base
                            if tc == NTC2 - 1:
                                w13_rel_at[slot] = (sHV, sHV.n - sHV.base)
                            ab_group += 1
                        jchunk += 1
                hv_target = sHV.n - sHV.base
                for dch in range(NDC4):
                    s0 = load_w2(e0, dch)
                    s1 = load_w2(e1, dch)
                    for tt in range(TT8):
                        ypar = y_group % 2
                        if py_fills[ypar] > 0:
                            W("tensor", sYEr[ypar], py_fills[ypar])
                        if tt == 0:
                            W("tensor", sW2r[s0], 16 * w2_fills[s0])
                            W("tensor", sW2r[s1], 16 * w2_fills[s1])
                            W("tensor", sHV, hv_target)
                        n_mm = 2 * NIC
                        mm_i = 0
                        for e01, ws in ((0, s0), (1, s1)):
                            for ic in range(NIC):
                                first = (mm_i == 0)
                                last = (mm_i == n_mm - 1)
                                E("tensor",
                                  lambda e, yp=ypar, e01=e01, ic=ic, tt=tt, ws=ws,
                                  f=first, l=last:
                                  e.matmul(ps[4 + yp][:, :],
                                           h_pair[:, e01, ic, tt * P:(tt + 1) * P],
                                           w2c[ws][:, ic, :], start=f, stop=l),
                                  [(sPYr[ypar], 1)] if last else ())
                                mm_i += 1
                        py_fills[ypar] += 1
                        yslot = y_group % 2
                        W("vector", sPYr[ypar], py_fills[ypar])
                        if ye_fills[yslot] > 0:
                            W("vector", sACCY, 16 * ye_rel_at[yslot])
                        E("vector",
                          lambda e, yp=ypar, ys=yslot:
                          e.tensor_copy(yev[ys][:, :], ps[4 + yp][:, :]),
                          [(sYEr[yslot], 1)])
                        ye_fills[yslot] += 1
                        W("gpsimd", sYEr[yslot], ye_fills[yslot])
                        row0 = half * HALF + tt * P
                        E("gpsimd",
                          lambda e, ys=yslot, r=row0, dch=dch:
                          e.dma_start(out=y_scr[r:r + P, dch * 512:(dch + 1) * 512],
                                      in_=yev[ys][:, :], accum_op=ALU.add),
                          [(sACCY, 16)])
                        ye_rel_at[yslot] = (sACCY.n - sACCY.base) // 16
                        if tt == TT8 - 1:
                            rel = (sYEr[ypar], sYEr[ypar].n - sYEr[ypar].base)
                            w2_rel_at[s0] = rel
                            w2_rel_at[s1] = rel
                        y_group += 1
                W("gpsimd", sACCY)
                MARK("gpsimd")
        pass
        sACCY_out = sACCY
    if _STAGE == "y":
        sFy = sem("sFy")
        W("sync", sACCY_out)
        E("sync", lambda e: e.dma_start(out=out, in_=y_scr), [(sFy, 16)])
        E("sync", lambda e: e.dma_start(out=sent, in_=sent_sb[:]), [(sFy, 16)])
        W("sync", sFy, 32)
        flush()
        st.close()
        return nc

    with contextlib.ExitStack() as stL:
        ecL = stL.enter_context
        u2 = [ecL(nc.sbuf_tensor("u2_%d" % i, [P, D], F32)) for i in (0, 1)]
        o_t = [ecL(nc.sbuf_tensor("o_t%d" % i, [P, D], F32)) for i in (0, 1)]
        sq2 = ecL(nc.sbuf_tensor("sq2", [P, D], F32))
        sum2 = ecL(nc.sbuf_tensor("sum2", [P, 1], F32))
        nm2 = ecL(nc.sbuf_tensor("nm2", [P, 1], F32))
        ssq2 = ecL(nc.sbuf_tensor("ssq2", [P, 1], F32))
        lnv2 = ecL(nc.sbuf_tensor("lnv2", [P, 1], F32))
        rstd2 = ecL(nc.sbuf_tensor("rstd2", [P, 1], F32))

        sU2 = [alias("sXT"), alias("sX1r")]
        u2_fills = [0, 0]
        sS5 = alias("sPSG")
        sOUT = alias("sWO")
        out_done = [0, 0]

        W("sync", sACCY_out)
        for tt in range(NTT):
            slot = tt % 2
            if u2_fills[slot] > 0:
                W("sync", sS5, tt - 1)
            E("sync",
              lambda e, s=slot, tt=tt:
              e.dma_start(out=u2[s][:], in_=y_scr[tt * P:(tt + 1) * P, :]),
              [(sU2[slot], 16)])
            u2_fills[slot] += 1
            W("scalar", sU2[slot], 16 * u2_fills[slot])
            if tt == 0:
                W("scalar", sC, 16 * 9)
            if out_done[slot] > 0:
                W("scalar", sOUT, 16 * out_done[slot])
            E("scalar", lambda e, s=slot:
              e.activation(sq2[:, :], u2[s][:, :], AF.Identity,
                           accum_out=sum2[:, 0:1]))
            DR("scalar")
            E("scalar", lambda e: e.mul(nm2[:, 0:1], sum2[:, 0:1], -1.0 / D))
            DR("scalar")
            E("scalar", lambda e, s=slot:
              e.activation(sq2[:, :], u2[s][:, :], AF.Square, bias=nm2[:, 0:1],
                           accum_out=ssq2[:, 0:1]))
            DR("scalar")
            E("scalar", lambda e:
              e.activation(lnv2[:, 0:1], ssq2[:, 0:1], AF.Ln, scale=1.0 / D,
                           bias=epsc_sb[:, 0:1]))
            DR("scalar")
            E("scalar", lambda e:
              e.activation(rstd2[:, 0:1], lnv2[:, 0:1], AF.Exp, scale=-0.5))
            DR("scalar")
            E("scalar", lambda e, s=slot:
              e.activation(sq2[:, :], u2[s][:, :], AF.Identity, bias=nm2[:, 0:1]))
            DR("scalar")
            E("scalar", lambda e, s=slot:
              e.activation(o_t[s][:, :], sq2[:, :], AF.Identity,
                           scale=rstd2[:, 0:1]))
            E("scalar", lambda e, s=slot:
              e.activation(o_t[s][:, :], sq2[:, :], AF.Identity,
                           scale=rstd2[:, 0:1]),
              [(sS5, 1)])
            W("sync", sS5, tt + 1)
            MARK()
            E("sync",
              lambda e, s=slot, tt=tt:
              e.dma_start(out=out[tt * P:(tt + 1) * P, :], in_=o_t[s][:, :]),
              [(sOUT, 16)])
            out_done[slot] = tt + 1
        W("sync", sOUT, 16 * NTT)
        sSE = sem("sSE") if "sSE" not in sems else sems["sSE"]
        E("sync", lambda e: e.dma_start(out=sent, in_=sent_sb[:]), [(sSE, 16)])
        W("sync", sSE, 16)
        flush()

    st.close()
    return nc



def _devfmt_w(wt):
    return np.ascontiguousarray(wt.reshape(16, P, 16, P).transpose(0, 3, 2, 1))


def make_in_maps(L0_attn_x, L0_attn_Wq, L0_attn_bq, L0_attn_Wk, L0_attn_bk,
                 L0_attn_Wv, L0_attn_bv, L0_attn_Wo, L0_attn_bo,
                 L0_ffn_Wg, L0_ffn_W1, L0_ffn_W3, L0_ffn_W2,
                 L0_ffn_Ws1, L0_ffn_Ws3, L0_ffn_Ws2):
    f32 = np.float32
    x = np.asarray(L0_attn_x, dtype=f32)
    Wq = np.asarray(L0_attn_Wq, f32); bq = np.asarray(L0_attn_bq, f32)
    Wk = np.asarray(L0_attn_Wk, f32); bk = np.asarray(L0_attn_bk, f32)
    Wv = np.asarray(L0_attn_Wv, f32); bv = np.asarray(L0_attn_bv, f32)
    Wo = np.asarray(L0_attn_Wo, f32); bo = np.asarray(L0_attn_bo, f32)
    Wg = np.asarray(L0_ffn_Wg, f32)
    W1 = np.asarray(L0_ffn_W1, f32); W3 = np.asarray(L0_ffn_W3, f32)
    W2 = np.asarray(L0_ffn_W2, f32)
    Ws1 = np.asarray(L0_ffn_Ws1, f32); Ws3 = np.asarray(L0_ffn_Ws3, f32)
    Ws2 = np.asarray(L0_ffn_Ws2, f32)

    wq_dev = _devfmt_w(Wq)
    wk_dev = _devfmt_w(Wk)
    wv_dev = _devfmt_w(Wv)
    wo_dev = np.ascontiguousarray(Wo.T.reshape(16, P, D).transpose(1, 0, 2))
    wg_dev = np.ascontiguousarray(Wg.T.reshape(16, P, 16).transpose(1, 0, 2))
    bq_l = np.ascontiguousarray(bq.reshape(16, P).T)
    bk_l = np.ascontiguousarray(bk.reshape(16, P).T)
    bv_l = np.ascontiguousarray(bv.reshape(16, P).T)
    bo_rep = np.ascontiguousarray(np.broadcast_to(bo, (P, D)))

    w1r = W1.reshape(16, NIC, P, 16, P).transpose(0, 1, 4, 3, 2)
    w1s = Ws1.reshape(2, NIC, P, 16, P).transpose(0, 1, 4, 3, 2)
    w1_dev = np.ascontiguousarray(np.concatenate([w1r, w1s], axis=0))
    w3r = W3.reshape(16, NIC, P, 16, P).transpose(0, 1, 4, 3, 2)
    w3s = Ws3.reshape(2, NIC, P, 16, P).transpose(0, 1, 4, 3, 2)
    w3_dev = np.ascontiguousarray(np.concatenate([w3r, w3s], axis=0))
    w2r = W2.reshape(16, NDC4, 512, NIC, P).transpose(0, 1, 4, 3, 2)
    w2s = Ws2.T.reshape(2, NIC, P, NDC4, 512).transpose(0, 3, 2, 1, 4)
    w2_dev = np.ascontiguousarray(np.concatenate([w2r, w2s], axis=0))

    ident = np.eye(P, dtype=f32)
    ones1 = np.ones((1, P), dtype=f32)
    epsc = np.full((P, 1), EPS, dtype=f32)
    sentv = np.full((P, 64), 123.0, dtype=f32)
    zpre = np.zeros((P, NDCH, NT), dtype=f32)

    shared = dict(
        wq_dev=wq_dev, wk_dev=wk_dev, wv_dev=wv_dev, wo_dev=wo_dev,
        bq_l=bq_l, bk_l=bk_l, bv_l=bv_l, bo_rep=bo_rep, wg_dev=wg_dev,
        w1_dev=w1_dev, w3_dev=w3_dev, w2_dev=w2_dev,
        ident=ident, ones1=ones1, epsc=epsc, sentv=sentv,
    )

    in_maps = []
    for c in range(NCORES):
        b, hh = divmod(c, 2)
        sl = slice(hh * NT, (hh + 1) * NT)
        xtm = np.ascontiguousarray(x[b, sl, :])
        xdev = np.ascontiguousarray(xtm.T.reshape(16, P, NT).transpose(1, 0, 2))
        if hh == 1:
            xpre = np.ascontiguousarray(
                x[b, 0:NT, :].T.reshape(16, P, NT).transpose(1, 0, 2))
            cm = np.ones((P, 1), dtype=f32)
        else:
            xpre = zpre
            cm = np.zeros((P, 1), dtype=f32)
        m = dict(shared)
        m.update(xdev=xdev, xtm=xtm, xpre=xpre, cmask=cm)
        in_maps.append(m)
    return in_maps


def kernel(**inputs):
    in_maps = make_in_maps(**inputs)
    nc = build_nc()
    res = run_bass_kernel_spmd(nc, in_maps, core_ids=list(range(NCORES)))
    full = np.empty((4, 4096, D), dtype=np.float32)
    for c in range(NCORES):
        b, hh = divmod(c, 2)
        full[b, hh * NT:(hh + 1) * NT, :] = res.results[c]["out"]
    return full
